# version 13
# speedup vs baseline: 1.1439x; 1.0212x over previous
import numpy as np
import ml_dtypes
from contextlib import ExitStack

import concourse.bass as bass
import concourse.mybir as mybir
from concourse.bass_utils import run_bass_kernel_spmd

F32 = mybir.dt.float32
BF16 = mybir.dt.bfloat16
AF = mybir.ActivationFunctionType
ALU = mybir.AluOpType
AX = mybir.AxisListType.X
_BF = ml_dtypes.bfloat16

B, S, D = 2, 4096, 2048
HD = 128
HPC = 4
QT = 1024
ND = D // 128
NT = S // 128
NRG = 4
NCH = HPC * 3
NE = 4
INV_SQRT_HD = 1.0 / np.sqrt(np.float64(HD))
MASK1 = np.float32(-9e15)
MASK2 = np.finfo(np.float32).min
EPS = 1e-5
INV_D = 1.0 / D
GROUPS = [[0, 1, 2, 3], [4, 5, 6, 7]]

ENGS = ("sync", "gpsimd", "vector", "scalar", "tensor")

STRICT = False
RELAX_ENGS = ("tensor", "scalar")


def _rope_tables():
    inv_freq = 1.0 / 10000.0 ** (np.arange(0, HD, 2, dtype=np.float64) / HD)
    t = np.arange(S, dtype=np.float64)
    emb = np.concatenate([np.outer(inv_freq, t)] * 2, axis=0)
    cos_e, sin_e = np.cos(emb), np.sin(emb)
    sgn = np.ones((HD, 1), dtype=np.float64)
    sgn[: HD // 2] = -1.0
    out = {}
    for a, (cu, su) in {1: (sin_e, cos_e), 2: (cos_e, sin_e)}.items():
        out[(a, "cq")] = np.ascontiguousarray((cu * INV_SQRT_HD).astype(np.float32))
        out[(a, "sq")] = np.ascontiguousarray((su * sgn * INV_SQRT_HD).astype(np.float32))
        out[(a, "ck")] = np.ascontiguousarray(cu.astype(np.float32))
        out[(a, "sk")] = np.ascontiguousarray((su * sgn).astype(np.float32))
    return out


def _rot_matrix():
    R = np.zeros((128, 128), dtype=np.float64)
    for i in range(64):
        R[i, i + 64] = 1.0
        R[i + 64, i] = 1.0
    return np.ascontiguousarray(R.T.astype(_BF))


def _causal_mask(val):
    m = np.zeros((128, 128), dtype=np.float32)
    m[np.triu_indices(128, k=1)] = val
    return m


class _Sem:
    __slots__ = ("name", "n", "h", "owner")

    def __init__(self, name, owner=None):
        self.name, self.n, self.h, self.owner = name, 0, None, owner


class _Plan:
    def __init__(self):
        self.ops = {e: [] for e in ENGS}
        self.own = {}

    def op(self, eng, fn=None, wait=(), inc=None, incn=1, dma=False):
        if STRICT:
            w = tuple((s, v) for (s, v) in wait if v > 0)
            if fn is not None and eng in self.own:
                o = self.own[eng]
                if o.n > 0:
                    w = ((o, o.n),) + w
        else:
            if eng in RELAX_ENGS:
                w = tuple((s, v) for (s, v) in wait
                          if v > 0 and s is not self.own.get(eng))
                if dma and fn is not None and eng in self.own:
                    o = self.own[eng]
                    if o.n > 0:
                        w = ((o, o.n),) + w
            else:
                w = tuple((s, v) for (s, v) in wait if v > 0)
                if fn is not None and eng in self.own:
                    o = self.own[eng]
                    if o.n > 0:
                        w = ((o, o.n),) + w
        if fn is None and inc is not None:
            raise RuntimeError("inc without instruction")
        if fn is not None or w:
            self.ops[eng].append((fn, w, inc, incn))
        if inc is not None:
            inc.n += incn
        return inc.n if inc is not None else None


def build_nc(dbg=None):
    nc = bass.Bass("TRN2", target_bir_lowering=False,
                   detect_race_conditions=STRICT)

    x_full = nc.dram_tensor("x_full", [S, D], F32, kind="ExternalInput")
    x_mine = nc.dram_tensor("x_mine", [QT, D], F32, kind="ExternalInput")
    wqkv1 = nc.dram_tensor("wqkv1", [D, NCH * 128], F32, kind="ExternalInput")
    bqkv1 = nc.dram_tensor("bqkv1", [128, NCH], F32, kind="ExternalInput")
    wom1 = nc.dram_tensor("wom1", [HPC * 128, D], F32, kind="ExternalInput")
    bo1 = nc.dram_tensor("bo1", [128, ND], F32, kind="ExternalInput")
    wqkv2 = nc.dram_tensor("wqkv2", [D, NCH * 128], F32, kind="ExternalInput")
    bqkv2 = nc.dram_tensor("bqkv2", [128, NCH], F32, kind="ExternalInput")
    wom2 = nc.dram_tensor("wom2", [HPC * 128, D], F32, kind="ExternalInput")
    bo2 = nc.dram_tensor("bo2", [128, ND], F32, kind="ExternalInput")
    ln1w = nc.dram_tensor("ln1w", [128, ND], F32, kind="ExternalInput")
    ln1b = nc.dram_tensor("ln1b", [128, ND], F32, kind="ExternalInput")
    ln2w = nc.dram_tensor("ln2w", [128, ND], F32, kind="ExternalInput")
    ln2b = nc.dram_tensor("ln2b", [128, ND], F32, kind="ExternalInput")
    wgate = nc.dram_tensor("wgate", [128, ND * NE], F32, kind="ExternalInput")
    we1 = nc.dram_tensor("we1", [NE, D, D], F32, kind="ExternalInput")
    be1 = nc.dram_tensor("be1", [128, NE * ND], F32, kind="ExternalInput")
    we2 = nc.dram_tensor("we2", [NE, D, D], F32, kind="ExternalInput")
    be2 = nc.dram_tensor("be2", [64, 128], F32, kind="ExternalInput")
    out_mine = nc.dram_tensor("out_mine", [QT, D], F32, kind="ExternalOutput")

    rs1_in = nc.dram_tensor("rs1_in", [4, 4, 512, QT], BF16)
    rs1_out = nc.dram_tensor("rs1_out", [D, QT], BF16)
    ag2_in = nc.dram_tensor("ag2_in", [D, QT], BF16)
    ag2_out = nc.dram_tensor("ag2_out", [4, 4, 512, QT], BF16)
    rs3_in = nc.dram_tensor("rs3_in", [4, 4, 512, QT], BF16)
    rs3_out = nc.dram_tensor("rs3_out", [D, QT], BF16)
    h1_sp = nc.dram_tensor("h1_sp", [D, QT], F32)
    h_sp = nc.dram_tensor("h_sp", [D, QT], F32)
    xg_sp = nc.dram_tensor("xg_sp", [D, QT], F32)

    dbg_out = None
    dsrc = None
    if dbg is not None:
        dbg_srcs = {"rs1": rs1_out, "h1": h1_sp, "xln1": ag2_in, "rs3": rs3_out,
                    "h": h_sp, "xg": xg_sp}
        dsrc = dbg_srcs[dbg]
        dbg_out = nc.dram_tensor("dbg_out", list(dsrc.shape), dsrc.dtype,
                                 kind="ExternalOutput")

    tabs = _rope_tables()
    c_tab = {k: nc.inline_tensor(v, name=f"tab{k[0]}{k[1]}") for k, v in tabs.items()}
    c_idb = nc.inline_tensor(np.eye(128, dtype=_BF), name="ident_bf")
    c_idf = nc.inline_tensor(np.eye(128, dtype=np.float32), name="ident_f32")
    c_rot = nc.inline_tensor(_rot_matrix(), name="rotT")
    c_m1 = nc.inline_tensor(_causal_mask(MASK1), name="mask1")
    c_m2 = nc.inline_tensor(_causal_mask(MASK2), name="mask2")
    c_onesc = nc.inline_tensor(np.ones((128, 1), dtype=_BF), name="ones_col")
    c_onesrf = nc.inline_tensor(np.ones((1, 128), dtype=np.float32), name="ones_rf")
    c_onesrb = nc.inline_tensor(np.ones((1, 128), dtype=_BF), name="ones_rb")
    c_vones = nc.inline_tensor(np.ones((128, NT * HPC), dtype=_BF),
                               name="v_ones")

    with ExitStack() as top:
        E = top.enter_context

        def sem(name, owner=None):
            s = _Sem(name, owner)
            s.h = E(nc.semaphore(name))
            return s

        sT = sem("sT", "tensor")
        sV = sem("sV", "vector")
        sS = sem("sS", "scalar")
        qW = [sem(f"qW{i}") for i in range(2)]
        qX = [sem(f"qX{i}") for i in range(2)]
        qR = sem("qR")
        qSp = [sem(f"qS{i}") for i in range(2)]
        qAG = sem("qAG")
        qGs = [sem(f"qAGs{i}") for i in range(2)]
        qRS = [sem(f"qRS{i}") for i in range(2)]
        qE = [sem(f"qE{i}") for i in range(8)]
        qXT = [sem(f"qXT{i}") for i in range(2)]
        qO = sem("qO")
        cc = sem("cc")
        qH = [sem(f"qH{i}") for i in range(2)]
        SEM_OF = {"sT": sT, "sV": sV, "sS": sS, "qS0": qSp[0], "qS1": qSp[1], "qAG": qAG, "qO": qO}

        sb_idb = E(nc.sbuf_tensor("sb_idb", [128, 128], BF16))
        sb_idf = E(nc.sbuf_tensor("sb_idf", [128, 128], F32))
        sb_rot = E(nc.sbuf_tensor("sb_rot", [128, 128], BF16))
        sb_m1 = E(nc.sbuf_tensor("sb_m1", [128, 128], F32))
        sb_m2 = E(nc.sbuf_tensor("sb_m2", [128, 128], F32))
        sb_onesc = E(nc.sbuf_tensor("sb_onesc", [128, 1], BF16))
        sb_onesrf = E(nc.sbuf_tensor("sb_onesrf", [1, 128], F32))
        sb_onesrb = E(nc.sbuf_tensor("sb_onesrb", [1, 128], BF16))
        sb_vo = E(nc.sbuf_tensor("sb_vo", [128, NT * HPC], BF16))
        sb_bqkv = E(nc.sbuf_tensor("sb_bqkv", [128, NCH], F32))
        sb_bo = E(nc.sbuf_tensor("sb_bo", [128, ND], F32))
        sb_ln1w = E(nc.sbuf_tensor("sb_ln1w", [128, ND], F32))
        sb_ln1b = E(nc.sbuf_tensor("sb_ln1b", [128, ND], F32))
        sb_ln2w = E(nc.sbuf_tensor("sb_ln2w", [128, ND], F32))
        sb_ln2b = E(nc.sbuf_tensor("sb_ln2b", [128, ND], F32))
        sb_wg = E(nc.sbuf_tensor("sb_wg", [128, ND * NE], F32))
        sb_be1 = E(nc.sbuf_tensor("sb_be1", [128, NE * ND], F32))
        sb_mu = E(nc.sbuf_tensor("sb_mu", [1, QT], F32))
        sb_iv = E(nc.sbuf_tensor("sb_iv", [1, QT], F32))
        sb_rw = E(nc.sbuf_tensor("sb_rw", [1, QT], F32))
        sb_cb = E(nc.sbuf_tensor("sb_cb", [128, QT], BF16))
        sb_sq = E(nc.sbuf_tensor("sb_sq", [128, QT], BF16))
        sb_mx2 = [E(nc.sbuf_tensor(f"sb_mx2{i}", [128, 8], F32)) for i in range(2)]
        sb_nm2 = [E(nc.sbuf_tensor(f"sb_nm2{i}", [128, 1], F32)) for i in range(2)]
        sb_sm2 = [E(nc.sbuf_tensor(f"sb_sm2{i}", [128, 8], F32)) for i in range(2)]
        sb_sum = E(nc.sbuf_tensor("sb_sum", [128, 1], F32))
        sb_rcp = E(nc.sbuf_tensor("sb_rcp", [128, 1], F32))
        sb_pt = [E(nc.sbuf_tensor(f"sb_pt{i}", [128, 512], BF16)) for i in range(2)]
        sb_qf = E(nc.sbuf_tensor("sb_qf", [128, 512], F32))
        sb_qb = E(nc.sbuf_tensor("sb_qb", [128, 512], BF16))
        sb_t1 = E(nc.sbuf_tensor("sb_t1", [128, 512], F32))
        sb_t2 = E(nc.sbuf_tensor("sb_t2", [128, 512], F32))
        sb_at = E(nc.sbuf_tensor("sb_at", [128, 128], BF16))

        P = _Plan()
        P.own = {"tensor": sT, "vector": sV, "scalar": sS}
        G = {}

        def barrier(keys, skip_engs=()):
            for key, semname in keys:
                G[key] = SEM_OF[semname].n
            fence = [(x, x.n) for x in
                     (sT, sV, sS, qW[0], qW[1], qX[0], qX[1], qR, qSp[0],
                      qSp[1], qAG, qGs[0], qGs[1], qRS[0], qRS[1], qO,
                      qH[0], qH[1], cc, *qE, qXT[0], qXT[1])]
            for e2 in ENGS:
                if e2 not in skip_engs:
                    P.op(e2, None, wait=fence)

        for dst, src in [
            (sb_idb, c_idb.ap()), (sb_idf, c_idf.ap()), (sb_rot, c_rot.ap()),
            (sb_m1, c_m1.ap()), (sb_m2, c_m2.ap()),
            (sb_onesc, c_onesc.ap()), (sb_onesrf, c_onesrf.ap()),
            (sb_onesrb, c_onesrb.ap()),
            (sb_bo, bo1[:, :]), (sb_ln1w, ln1w[:, :]), (sb_ln1b, ln1b[:, :]),
            (sb_ln2w, ln2w[:, :]), (sb_ln2b, ln2b[:, :]),
            (sb_wg, wgate[:, :]), (sb_be1, be1[:, :]),
            (sb_bqkv, bqkv1[:, :]), (sb_vo, c_vones.ap()),
        ]:
            P.op("sync", lambda e, d=dst, s=src: e.dma_start(out=d[:], in_=s),
                 inc=qX[0], incn=16)
        G["consts"] = qX[0].n
        for _e in ENGS:
            P.op(_e, None, wait=[(qX[0], G["consts"])])

        def ln_stats_chunk(stp, i, src_ap, depv):
            (su0, su1, sq0, sq1) = stp
            P.op("vector", lambda e, d=src_ap: e.tensor_copy(sb_cb[:], d),
                 wait=[depv, (sT, G.get("stT", 0))], inc=sV)
            cbv = sV.n
            P.op("scalar", lambda e, d=src_ap: e.activation(sb_sq[:], d, AF.Square),
                 wait=[depv, (sT, G.get("stT", 0))], inc=sS)
            sqv = sS.n
            first, last = (i == 0), (i == ND - 1)
            for hh, (psu, psq) in enumerate(((su0, sq0), (su1, sq1))):
                w = [(sV, cbv), (sS, sqv)]
                if first and hh == 0:
                    w.append((sV, G.get("statfree", 0)))
                P.op("tensor",
                     lambda e, hh=hh, p=psu, st=first, sp=last: e.matmul(
                         p[0:1, 0:512], sb_onesc[:, 0:1],
                         sb_cb[:, hh * 512:(hh + 1) * 512], start=st, stop=sp),
                     wait=(w if hh == 0 else ()), inc=None)
                P.op("tensor",
                     lambda e, hh=hh, p=psq, st=first, sp=last: e.matmul(
                         p[0:1, 0:512], sb_onesc[:, 0:1],
                         sb_sq[:, hh * 512:(hh + 1) * 512], start=st, stop=sp),
                     inc=(sT if hh == 1 else None))
            G["stT"] = sT.n
            return cbv, sqv

        def ln_finish(stp):
            (su0, su1, sq0, sq1) = stp
            mmv = sT.n
            for hh, psu in ((0, su0), (1, su1)):
                P.op("vector",
                     lambda e, hh=hh, p=psu: e.tensor_scalar_mul(
                         sb_mu[0:1, hh * 512:(hh + 1) * 512], p[0:1, 0:512], INV_D),
                     wait=[(sT, mmv)], inc=sV)
            for hh, psq in ((0, sq0), (1, sq1)):
                P.op("vector",
                     lambda e, hh=hh, p=psq: e.tensor_scalar_mul(
                         sb_rw[0:1, hh * 512:(hh + 1) * 512], p[0:1, 0:512], INV_D),
                     wait=[(sT, mmv)], inc=sV)
            P.op("vector", lambda e: e.tensor_mul(sb_iv[0:1, :], sb_mu[0:1, :],
                                                  sb_mu[0:1, :]), inc=sV)
            P.op("vector", lambda e: e.tensor_sub(sb_rw[0:1, :], sb_rw[0:1, :],
                                                  sb_iv[0:1, :]), inc=sV)
            P.op("vector", lambda e: e.tensor_scalar_add(sb_rw[0:1, :],
                                                         sb_rw[0:1, :], EPS),
                 inc=sV)
            G["statfree"] = vv = sV.n
            P.op("scalar", lambda e: e.activation(sb_iv[0:1, :], sb_rw[0:1, :],
                                                  AF.Ln),
                 wait=[(sV, vv)], inc=sS)
            P.op("scalar", lambda e: e.activation(sb_iv[0:1, :], sb_iv[0:1, :],
                                                  AF.Exp, scale=-0.5), inc=sS)
            return sS.n

        def emit_bcasts(ps_bc, ivv):
            for hh in range(2):
                P.op("tensor",
                     lambda e, hh=hh, p=ps_bc: e.matmul(
                         p[:, 0:512], sb_onesrf[0:1, :],
                         sb_mu[0:1, hh * 512:(hh + 1) * 512], start=True, stop=True),
                     wait=[(sS, ivv), (sV, G.get("bcfree", 0)), (sV, sV.n)],
                     inc=sT)
                tv = sT.n
                P.op("vector",
                     lambda e, hh=hh, p=ps_bc, d=G["sb_bcm"][hh]: e.tensor_copy(
                         d[:], p[:, 0:512]),
                     wait=[(sT, tv)], inc=sV)
                P.op("tensor",
                     lambda e, hh=hh, p=ps_bc: e.matmul(
                         p[:, 0:512], sb_onesrf[0:1, :],
                         sb_iv[0:1, hh * 512:(hh + 1) * 512], start=True, stop=True),
                     wait=[(sV, sV.n)], inc=sT)
                tv = sT.n
                P.op("vector",
                     lambda e, hh=hh, p=ps_bc, d=G["sb_bci"][hh]: e.tensor_copy(
                         d[:], p[:, 0:512]),
                     wait=[(sT, tv)], inc=sV)
            return sV.n

        def norm_apply(dst_ap, src_ap, bcv, extra_wait=()):
            for hh in range(2):
                P.op("vector",
                     lambda e, hh=hh, d=dst_ap, s=src_ap, bm=G["sb_bcm"][hh]:
                     e.tensor_sub(
                         d[:, hh * 512:(hh + 1) * 512],
                         s[:, hh * 512:(hh + 1) * 512], bm[:]),
                     wait=[(sV, bcv)] + list(extra_wait), inc=sV)
                P.op("vector",
                     lambda e, hh=hh, d=dst_ap, bi=G["sb_bci"][hh]: e.tensor_mul(
                         d[:, hh * 512:(hh + 1) * 512],
                         d[:, hh * 512:(hh + 1) * 512], bi[:]),
                     inc=sV)
            G["bcfree"] = sV.n
            return sV.n

        def qkv_proj(a, sbQ, sbK, sbV, ps, scope):
            barrier([("pb0", "sS"), ("pb1", "sS"), ("pbv0", "sV"), ("pbv1", "sV"),
                     ("vtT", "sT"), ("vt0", "sV"), ("vt1", "sV"),
                     ("xtp0", "sV"), ("xtp1", "sV"), ("rotb", "sV"),
                     ("ropeb", "sV"), ("qbT", "sT"), ("ropedone", "sV"),
                     ("xbf_done", "sV"), ("xTfree", "sT")])
            vod = sbV[:].rearrange("p (x c) -> p x c", c=129)[
                :, :, 128:129].rearrange("p x c -> p (x c)")
            P.op("vector",
                 lambda e, d=vod: e.tensor_copy(d, sb_vo[:]),
                 wait=[(qX[0], G["consts"]), (sV, sV.n)], inc=sV)
            G["vones"] = sV.n
            A = scope.enter_context
            wsrc = wqkv1 if a == 1 else wqkv2
            if a == 2:
                P.op("sync", lambda e: e.dma_start(out=sb_bqkv[:], in_=bqkv2[:, :]),
                     wait=[(sS, sS.n)], inc=qX[0], incn=16)
            biasv = qX[0].n

            sb_tab = A(nc.sbuf_tensor(f"sb_tab{a}", [128, 4096], F32))
            sb_wc = [A(nc.sbuf_tensor(f"sb_wc{a}{i}", [128, ND * 128], BF16))
                     for i in range(2)]
            if a == 1:
                sb_xtm = [A(nc.sbuf_tensor(f"sb_xtm{i}", [128, D], BF16))
                          for i in range(2)]
            sb_xT = A(nc.sbuf_tensor(f"sb_xT{a}", [128, ND * 1024], BF16))
            assert nc.sbuf_bytes_remaining >= 0, "sbuf overflow qkv"

            wready = {}
            wguard = {0: 0, 1: 0}

            def load_wchunk(c):
                sl = c % 2
                for g in range(4):
                    src = wsrc[g * 512:(g + 1) * 512, c * 128:(c + 1) * 128]
                    src = src.rearrange("(dc p) j -> p dc j", p=128)
                    dst = sb_wc[sl][:].rearrange("p (dc j) -> p dc j", j=128)[
                        :, g * 4:(g + 1) * 4]
                    P.op("gpsimd",
                         lambda e, d=dst, ss=src: e.dma_start(out=d, in_=ss),
                         wait=([(sT, wguard[sl])] if g == 0 else ()),
                         inc=qW[sl], incn=16)
                wready[c] = qW[sl].n

            def load_tables(r):
                for i, k in enumerate(("cq", "sq", "ck", "sk")):
                    src = c_tab[(a, k)][:, r * 1024:(r + 1) * 1024]
                    dst = sb_tab[:, i * 1024:(i + 1) * 1024]
                    P.op("sync",
                         lambda e, d=dst, ss=src: e.dma_start(out=d, in_=ss),
                         wait=([(sV, G.get("ropedone", 0))] if i == 0 else ()),
                         inc=qR, incn=16)
                return qR.n

            xready = {}

            def load_rhs(r):
                if a == 2:
                    xr = []
                    for g in range(4):
                        src = ag2_out[g, r, :, :].rearrange(
                            "(dc p) t -> p dc t", p=128)
                        dst = sb_xT[:].rearrange("p (dc t) -> p dc t", t=1024)[
                            :, g * 4:(g + 1) * 4, :]
                        P.op("sync",
                             lambda e, d=dst, ss=src: e.dma_start(out=d, in_=ss),
                             wait=[(sT, G.get("xTfree", 0)),
                                   (cc, G["cc_ag2"][g]),
                                   (qX[1], qX[1].n)],
                             inc=qX[1], incn=16)
                        xr.append((qX[1], qX[1].n))
                    xready[r] = xr
                    return
                done = 0
                for tt in range(8):
                    row = r * 1024 + tt * 128
                    xsl = tt % 2
                    P.op("gpsimd",
                         lambda e, row=row, xsl=xsl: e.dma_start(
                             out=sb_xtm[xsl][:], in_=x_full[row:row + 128, :]),
                         wait=[(sT, G.get(f"xtm{xsl}", 0))],
                         inc=qXT[xsl], incn=16)
                    ldv = qXT[xsl].n
                    for bat in range(2):
                        pt = ps["t0"] if bat == 0 else ps["t1"]
                        for j in range(8):
                            dc = bat * 8 + j
                            P.op("tensor",
                                 lambda e, dc=dc, pt=pt, j=j, xsl=xsl: e.transpose(
                                     pt[:, j * 128:(j + 1) * 128],
                                     sb_xtm[xsl][:, dc * 128:(dc + 1) * 128],
                                     sb_idb[:]),
                                 wait=([(qXT[xsl], ldv),
                                        (sV, G.get(f"xtp{bat}", 0)),
                                        (sV, G.get(f"vt{bat}", 0)),
                                        (sT, G.get("xTfree", 0))] if j == 0 else ()),
                                 inc=(sT if j == 7 else None))
                        tpv = sT.n
                        G[f"xtm{xsl}"] = tpv
                        dst = sb_xT[:].rearrange("p (dc t) -> p dc t", t=1024)[
                            :, bat * 8:(bat + 1) * 8, tt * 128:(tt + 1) * 128]
                        src2 = pt[:].rearrange("p (dc j) -> p dc j", j=128)
                        P.op("vector",
                             lambda e, d=dst, ss=src2: e.tensor_copy(d, ss),
                             wait=[(sT, tpv)], inc=sV)
                        G[f"xtp{bat}"] = done = sV.n
                xready[r] = (sV, done)

            for r in range(NRG):
                G["xTfree"] = sT.n
                load_rhs(r)
                tabv = load_tables(r)
                wguard[0] = wguard[1] = sT.n
                load_wchunk(0)
                load_wchunk(1)
                for c in range(NCH):
                    head, kindi = c // 3, c % 3
                    for hf in range(2):
                        par = hf
                        pb = ps["a"] if par == 0 else ps["b"]
                        for dc in range(ND):
                            P.op("tensor",
                                 lambda e, pb=pb, sl=c % 2, dc=dc, hf=hf: e.matmul(
                                     pb[:],
                                     sb_wc[sl][:, dc * 128:(dc + 1) * 128],
                                     sb_xT[:, dc * 1024 + hf * 512:
                                           dc * 1024 + (hf + 1) * 512],
                                     start=(dc == 0), stop=(dc == ND - 1)),
                                 wait=(([(qW[c % 2], wready[c]),
                                         (xready[r][dc // 4] if a == 2
                                          else xready[r]),
                                         (sS, G.get(f"pb{par}", 0)),
                                         (sV, G.get(f"pbv{par}", 0)),
                                         (qX[0], biasv)]) if dc == 0 else
                                       ([xready[r][dc // 4]]
                                        if a == 2 and dc % 4 == 0 else ())),
                                 inc=(sT if dc == ND - 1 else None))
                        mmv = sT.n
                        if hf == 1 and c + 2 < NCH:
                            wguard[c % 2] = mmv
                            load_wchunk(c + 2)
                        tok0 = r * 1024 + hf * 512
                        if kindi < 2:
                            dst = (sbQ if kindi == 0 else sbK)[
                                :, head * S + tok0: head * S + tok0 + 512]
                            ti = 0 if kindi == 0 else 2
                            cos_ap = sb_tab[:, ti * 1024 + hf * 512:
                                            ti * 1024 + (hf + 1) * 512]
                            sin_ap = sb_tab[:, (ti + 1) * 1024 + hf * 512:
                                            (ti + 1) * 1024 + (hf + 1) * 512]
                            P.op("scalar",
                                 lambda e, p=pb, c=c: e.activation(
                                     sb_qf[:], p[:], AF.Identity,
                                     bias=sb_bqkv[:, c:c + 1], scale=1.0),
                                 wait=[(sT, mmv), (sV, G.get("ropeb", 0))], inc=sS)
                            G[f"pb{par}"] = ev = sS.n
                            P.op("vector",
                                 lambda e: e.tensor_copy(sb_qb[:], sb_qf[:]),
                                 wait=[(sS, ev), (qR, tabv),
                                       (sT, G.get("vtT", 0))], inc=sV)
                            cv = sV.n
                            P.op("tensor",
                                 lambda e: e.matmul(ps["r"][:], sb_rot[:], sb_qb[:],
                                                    start=True, stop=True),
                                 wait=[(sV, cv), (sV, G.get("rotb", 0))], inc=sT)
                            G["qbT"] = rv = sT.n
                            P.op("vector",
                                 lambda e, cos=cos_ap: e.tensor_mul(
                                     sb_t1[:], sb_qf[:], cos), inc=sV)
                            P.op("vector",
                                 lambda e, sin=sin_ap: e.tensor_mul(
                                     sb_t2[:], ps["r"][:], sin),
                                 wait=[(sT, rv)], inc=sV)
                            G["rotb"] = sV.n
                            P.op("vector",
                                 lambda e, d=dst: e.tensor_add(d, sb_t1[:],
                                                               sb_t2[:]),
                                 inc=sV)
                            G["ropeb"] = G["ropedone"] = G[f"pbv{par}"] = sV.n
                        else:
                            P.op("scalar",
                                 lambda e, p=pb, c=c: e.activation(
                                     sb_qb[:], p[:], AF.Identity,
                                     bias=sb_bqkv[:, c:c + 1], scale=1.0),
                                 wait=[(sT, mmv), (sT, G.get("vtT", 0)),
                                       (sT, G.get("qbT", 0))], inc=sS)
                            G[f"pb{par}"] = ev = sS.n
                            pt = ps["t0"] if hf == 0 else ps["t1"]
                            for tt in range(4):
                                P.op("tensor",
                                     lambda e, pt=pt, tt=tt: e.transpose(
                                         pt[:, tt * 128:(tt + 1) * 128],
                                         sb_qb[:, tt * 128:(tt + 1) * 128],
                                         sb_idb[:]),
                                     wait=([(sS, ev), (sV, G.get(f"vt{hf}", 0)),
                                            (sV, G.get(f"xtp{hf}", 0))]
                                           if tt == 0 else ()),
                                     inc=(sT if tt == 3 else None))
                            G["vtT"] = tpv = sT.n
                            t4 = tok0 // 128
                            dst = sbV[:].rearrange("p (t hc) -> p t hc",
                                                   hc=HPC * 129)[
                                :, t4:t4 + 4, head * 129: head * 129 + 128]
                            src2 = pt[:].rearrange("p (t j) -> p t j", j=128)[
                                :, 0:4, :]
                            P.op("vector",
                                 lambda e, d=dst, ss=src2: e.tensor_copy(d, ss),
                                 wait=[(sT, tpv)], inc=sV)
                            G[f"vt{hf}"] = G[f"pbv{par}"] = sV.n

        def attention(a, sbQ, sbK, sbV, sbAT, sbP, sbSC, ps):
            barrier([("qks0", "sS"), ("qks1", "sS"), ("aev0", "sV"),
                     ("aev1", "sV"), ("atT", "sT"), ("atE", "sV")])
            mask = sb_m1 if a == 1 else sb_m2
            gch = [0]
            IT = [(h, t) for h in range(HPC) for t in range(NT)]
            st = {}

            def emit_qk(idx):
                h, t = IT[idx]
                par = idx % 2
                klen = (t + 1) * 128
                nchunks = (klen + 511) // 512
                lhs = sbQ[:, h * S + t * 128: h * S + (t + 1) * 128]
                evs = []
                for ci in range(nchunks):
                    g = gch[0]; gch[0] += 1
                    pb = ps["a"] if g % 2 == 0 else ps["b"]
                    w = klen - ci * 512 if ci == nchunks - 1 else 512
                    rhs = sbK[:, h * S + ci * 512: h * S + ci * 512 + w]
                    P.op("tensor",
                         lambda e, pb=pb, l=lhs, r=rhs, w=w: e.matmul(
                             pb[:, 0:w], l, r, start=True, stop=True),
                         wait=[(sS, st.get(("ev", g - 2), 0))],
                         inc=sT)
                    mmv = sT.n
                    P.op("scalar",
                         lambda e, pb=pb, ci=ci, w=w, par=par: e.copy(
                             sbSC[par][:, ci * 512: ci * 512 + w], pb[:, 0:w]),
                         wait=[(sT, mmv), (sS, st.get(("rowfree", par), 0))],
                         inc=sS)
                    st[("ev", g)] = sS.n
                    evs.append(sS.n)
                st[("evs", idx)] = evs

            def emit_max(idx):
                h, t = IT[idx]
                par = idx % 2
                klen = (t + 1) * 128
                nchunks = (klen + 511) // 512
                evs = st.pop(("evs", idx))
                for ci in range(nchunks - 1):
                    P.op("vector",
                         lambda e, ci=ci, par=par: e.reduce_max(
                             sb_mx2[par][:, ci:ci + 1],
                             sbSC[par][:, ci * 512: ci * 512 + 512], axis=AX),
                         wait=[(sS, evs[ci])] +
                              ([(sV, st.get(("mxfree", par), 0))]
                               if ci == 0 else []),
                         inc=sV)
                P.op("vector",
                     lambda e, t=t, m=mask, par=par: e.tensor_add(
                         sbSC[par][:, t * 128:(t + 1) * 128],
                         sbSC[par][:, t * 128:(t + 1) * 128], m[:]),
                     wait=[(sS, evs[-1])] +
                          ([(sV, st.get(("mxfree", par), 0))]
                           if nchunks == 1 else []),
                     inc=sV)
                mkv = sV.n
                ci = nchunks - 1
                w = klen - ci * 512
                P.op("vector",
                     lambda e, ci=ci, w=w, par=par: e.reduce_max(
                         sb_mx2[par][:, ci:ci + 1],
                         sbSC[par][:, ci * 512: ci * 512 + w], axis=AX),
                     wait=[(sV, mkv)], inc=sV)
                st[("mx", idx)] = sV.n

            def emit_soft(idx):
                h, t = IT[idx]
                par = idx % 2
                klen = (t + 1) * 128
                nchunks = (klen + 511) // 512
                P.op("vector",
                     lambda e, n=nchunks, par=par: e.reduce_max(
                         sb_nm2[par][:, 0:1], sb_mx2[par][:, 0:n], axis=AX,
                         negate=True),
                     wait=[(sV, st[("mx", idx)])], inc=sV)
                st[("mxfree", par)] = gmv = sV.n
                expch = []
                for ci in range(nchunks):
                    w = klen - ci * 512 if ci == nchunks - 1 else 512
                    P.op("scalar",
                         lambda e, ci=ci, w=w, par=par: e.activation(
                             sbP[par][:, ci * 512: ci * 512 + w],
                             sbSC[par][:, ci * 512: ci * 512 + w],
                             AF.Exp, bias=sb_nm2[par][:, 0:1], scale=1.0),
                         wait=[(sV, gmv), (sT, st.get(("Pfree", par), 0))],
                         inc=sS)
                    expch.append(sS.n)
                st[("rowfree", par)] = sS.n
                st[("soft", idx)] = (expch, 0)

            def emit_pv(idx):
                h, t = IT[idx]
                par = idx % 2
                klen = (t + 1) * 128
                nchunks = (klen + 511) // 512
                expch, rcpv = st.pop(("soft", idx))
                acc = ps["acc0"] if t % 2 == 0 else ps["acc1"]
                nbat = (t + 4) // 4
                ptev = {}

                def tp_batch(b):
                    pt = ps["t0"] if b % 2 == 0 else ps["t1"]
                    n = min(4, (t + 1) - b * 4)
                    for j in range(n):
                        kt = b * 4 + j
                        P.op("tensor",
                             lambda e, pt=pt, j=j, kt=kt, par=par: e.transpose(
                                 pt[:, j * 128:(j + 1) * 128],
                                 sbP[par][:, kt * 128:(kt + 1) * 128], sb_idb[:]),
                             wait=([(sS, expch[min(b, nchunks - 1)]),
                                    ptev.get(b - 2, (sS, 0))]
                                   if j == 0 else ()),
                             inc=(sT if j == n - 1 else None))
                    tv = sT.n
                    if b % 2 == 0:
                        P.op("scalar",
                             lambda e, pt=pt, b=b, n=n: e.copy(
                                 sb_pt[b % 2][:, 0:n * 128], pt[:, 0:n * 128]),
                             wait=[(sT, tv)], inc=sS)
                        ptev[b] = (sS, sS.n)
                    else:
                        P.op("vector",
                             lambda e, pt=pt, b=b, n=n: e.tensor_copy(
                                 sb_pt[b % 2][:, 0:n * 128], pt[:, 0:n * 128]),
                             wait=[(sT, tv)], inc=sV)
                        ptev[b] = (sV, sV.n)

                def pv_batch(b):
                    n = min(4, (t + 1) - b * 4)
                    for j in range(n):
                        kt = b * 4 + j
                        vtile = sbV[:, kt * (HPC * 129) + h * 129:
                                    kt * (HPC * 129) + h * 129 + 129]
                        P.op("tensor",
                             lambda e, acc=acc, sl=sb_pt[b % 2], j=j, v=vtile,
                             stt=(b == 0 and j == 0), sp=(kt == t): e.matmul(
                                 acc[:, 0:129], sl[:, j * 128:(j + 1) * 128],
                                 v, start=stt, stop=sp),
                             wait=([ptev[b]] +
                                   ([(sV, st.get(("aev", t % 2), 0)),
                                     (sV, G["vones"])]
                                    if b == 0 and j == 0 else []))
                             if j == 0 else (),
                             inc=(sT if kt == t else None))

                tp_batch(0)
                for bb in range(1, nbat):
                    tp_batch(bb)
                    pv_batch(bb - 1)
                pv_batch(nbat - 1)
                st[("Pfree", par)] = pvv = sT.n
                P.op("vector",
                     lambda e, acc=acc: e.reciprocal(
                         sb_rcp[:, 0:1], acc[:, 128:129]),
                     wait=[(sT, pvv)], inc=sV)
                rcv = sV.n
                P.op("vector",
                     lambda e, acc=acc: e.tensor_scalar_mul(
                         sb_at[:], acc[:, 0:128], sb_rcp[:, 0:1]),
                     wait=[(sV, rcv), (sT, st.get("atT", 0))],
                     inc=sV)
                st[("aev", t % 2)] = av = sV.n
                P.op("tensor",
                     lambda e: e.transpose(ps["t2"][:, 0:128], sb_at[:],
                                           sb_idb[:]),
                     wait=[(sV, av), (sV, st.get("atE", 0))], inc=sT)
                st["atT"] = atv = sT.n
                P.op("vector",
                     lambda e, h=h, t=t, d=sbAT: e.tensor_copy(
                         d[:, h * S + t * 128: h * S + (t + 1) * 128],
                         ps["t2"][:, 0:128]),
                     wait=[(sT, atv)], inc=sV)
                st["atE"] = sV.n

            for idx in range(len(IT) + 1):
                if idx >= 1:
                    emit_soft(idx - 1)
                if idx < len(IT):
                    emit_qk(idx)
                    emit_max(idx)
                if idx >= 1:
                    emit_pv(idx - 1)

        def wo_partial(a, sbAT, ps, scope):
            barrier([("wp0", "sS"), ("wp1", "sS")])
            G["wstq0"] = qGs[0].n
            G["wstq1"] = qGs[1].n
            A = scope.enter_context
            wsrc = wom1 if a == 1 else wom2
            rsin = rs1_in if a == 1 else rs3_in
            sb_wom = A(nc.sbuf_tensor(f"sb_wom{a}", [128, HPC * D], BF16))
            sb_wst = [A(nc.sbuf_tensor(f"sb_wst{a}{i}", [128, 4096], BF16))
                      for i in range(2)]
            assert nc.sbuf_bytes_remaining >= 0, "sbuf overflow wo_partial"
            for dc in range(HPC):
                src = wsrc[dc * 128:(dc + 1) * 128, :]
                dst = sb_wom[:, dc * D:(dc + 1) * D]
                P.op("gpsimd",
                     lambda e, d=dst, s=src: e.dma_start(out=d, in_=s),
                     inc=qW[0], incn=16)
            womv = qW[0].n
            atv = sV.n
            for i in range(ND):
                sl = i % 2
                for rng in range(8):
                    par = rng % 2
                    pb = ps["a"] if par == 0 else ps["b"]
                    for dc in range(HPC):
                        lhs_ap = sb_wom[:, dc * D + i * 128: dc * D + (i + 1) * 128]
                        rhs = sbAT[:, dc * S + rng * 512: dc * S + (rng + 1) * 512]
                        P.op("tensor",
                             lambda e, pb=pb, l=lhs_ap, r=rhs, dc=dc: e.matmul(
                                 pb[:], l, r, start=(dc == 0), stop=(dc == HPC - 1)),
                             wait=([(qW[0], womv), (sV, atv),
                                    (sS, G.get(f"wp{par}", 0))] if dc == 0 else ()),
                             inc=(sT if dc == HPC - 1 else None))
                    mmv = sT.n
                    P.op("scalar",
                         lambda e, pb=pb, sl=sl, rng=rng: e.copy(
                             sb_wst[sl][:, rng * 512:(rng + 1) * 512], pb[:]),
                         wait=[(sT, mmv)] +
                              ([(qGs[sl], G.get(f"wstq{sl}", 0))] if rng == 0
                               else []),
                         inc=sS)
                    G[f"wp{par}"] = sS.n
                stv = sS.n
                src = sb_wst[sl][:].rearrange("p (q f) -> p q f", q=4)
                dst = rsin[i // 4, :, (i % 4) * 128:(i % 4 + 1) * 128, :]
                dst = dst.rearrange("q p f -> p q f")
                P.op("sync",
                     lambda e, d=dst, s=src: e.dma_start(out=d, in_=s),
                     wait=[(sS, stv)], inc=qGs[sl], incn=16)
                G[f"wstq{sl}"] = qGs[sl].n
                if i % 4 == 3:
                    G[f"wosub{i // 4}"] = [(qGs[0], qGs[0].n),
                                           (qGs[1], qGs[1].n)]

        def collective(kind, cin, cout, key, subwaits=None):
            vals = []
            for g in range(4):
                if kind == "ReduceScatter":
                    ins = cin[g]
                    outs = cout[g * 512:(g + 1) * 512, :]
                else:
                    ins = cin[g * 512:(g + 1) * 512, :]
                    outs = cout[g]
                w = (subwaits[g] if subwaits else
                     [(qAG, qAG.n), (qGs[0], qGs[0].n), (qGs[1], qGs[1].n),
                      (qSp[0], qSp[0].n), (qSp[1], qSp[1].n)])
                P.op("gpsimd",
                     lambda e, ii=ins, oo=outs, k=kind: e.collective_compute(
                         k, ALU.add if k == "ReduceScatter" else ALU.bypass,
                         replica_groups=GROUPS,
                         ins=[ii.opt()], outs=[oo.opt()]),
                     wait=w, inc=cc)
                vals.append(cc.n)
            G[key] = vals

        def res_ln(which, scope):
            barrier([("rsb0", "sS"), ("rsb1", "sS"), ("chq0", "qS0"),
                     ("chq1", "qS1"), ("chu0", "sV"), ("chu1", "sV"),
                     ("xtT", "sT"), ("xev", "sV"), ("h1u0", "sV"),
                     ("h1u1", "sV"), ("statfree", "sV"), ("bcfree", "sV"),
                     ("stT", "sT")])
            A = scope.enter_context
            rsout = rs1_out if which == 1 else rs3_out
            spill = h1_sp if which == 1 else h_sp
            sb_ch = [A(nc.sbuf_tensor(f"sb_ch{which}{i}", [128, QT], F32))
                     for i in range(2)]
            sb_hf = A(nc.sbuf_tensor(f"sb_hf{which}", [128, ND * QT], F32))
            sb_h1s = [A(nc.sbuf_tensor(f"sb_h1s{which}{i}", [128, QT], F32))
                      for i in range(2)]
            sb_rsb = [A(nc.sbuf_tensor(f"sb_rsb{which}{i}", [128, QT], BF16))
                      for i in range(2)]
            ps_x = A(nc.psum_tensor(f"r{which}_x", [128, 512], F32))
            ps_s = [A(nc.psum_tensor(f"r{which}_s{i}", [128, 512], F32))
                    for i in range(4)]
            ps_bc = A(nc.psum_tensor(f"r{which}_bc", [128, 512], F32))
            G["sb_bcm"] = [A(nc.sbuf_tensor(f"sb_bcm{which}{i}", [128, 512], F32))
                           for i in range(2)]
            G["sb_bci"] = [A(nc.sbuf_tensor(f"sb_bci{which}{i}", [128, 512], F32))
                           for i in range(2)]
            stp = tuple(ps_s)
            assert nc.sbuf_bytes_remaining >= 0, "sbuf overflow res_ln"

            if which == 2:
                P.op("sync", lambda e: e.dma_start(out=sb_bo[:], in_=bo2[:, :]),
                     wait=[(sS, sS.n)], inc=qX[0], incn=16)
            bov = qX[0].n
            ccn = G["cc_rs1"] if which == 1 else G["cc_rs3"]

            if which == 1:
                sb_xs = [A(nc.sbuf_tensor(f"sb_xs{i}", [128, D], F32))
                         for i in range(2)]
                sb_xmT = A(nc.sbuf_tensor("sb_xmT", [128, ND * QT], F32))
                assert nc.sbuf_bytes_remaining >= 0, "sbuf overflow xprep"
                for tt in range(8):
                    xsl = tt % 2
                    P.op("sync",
                         lambda e, tt=tt, xsl=xsl: e.dma_start(
                             out=sb_xs[xsl][:],
                             in_=x_mine[tt * 128:(tt + 1) * 128, :]),
                         wait=[(sT, G.get(f"xpt{xsl}", 0))],
                         inc=qH[xsl], incn=16)
                    xv = qH[xsl].n
                    for g4 in range(4):
                        for j in range(4):
                            dc = g4 * 4 + j
                            P.op("tensor",
                                 lambda e, xsl=xsl, dc=dc, j=j: e.transpose(
                                     ps_x[:, j * 128:(j + 1) * 128],
                                     sb_xs[xsl][:, dc * 128:(dc + 1) * 128],
                                     sb_idf[:]),
                                 wait=([(qH[xsl], xv),
                                        (sV, G.get("xpe", 0))] if j == 0 else ()),
                                 inc=(sT if j == 3 else None))
                        tv = sT.n
                        dst = sb_xmT[:].rearrange(
                            "p (dc t) -> p dc t", t=QT)[
                            :, g4 * 4:(g4 + 1) * 4, tt * 128:(tt + 1) * 128]
                        src2 = ps_x[:].rearrange("p (j c) -> p j c", c=128)
                        P.op("vector",
                             lambda e, d=dst, ss=src2: e.tensor_copy(d, ss),
                             wait=[(sT, tv)], inc=sV)
                        G["xpe"] = sV.n
                    G[f"xpt{xsl}"] = sT.n
                xprep_done = sV.n

            for i in range(ND):
                sl = i % 2
                P.op("sync",
                     lambda e, i=i, sl=sl, r=rsout: e.dma_start(
                         out=sb_rsb[sl][:], in_=r[i * 128:(i + 1) * 128, :]),
                     wait=[(cc, ccn[i // 4]), (sS, G.get(f"rsb{sl}", 0))],
                     inc=qRS[sl], incn=16)
                rv = qRS[sl].n
                h1v = 0
                if which == 2:
                    P.op("sync",
                         lambda e, i=i, sl=sl: e.dma_start(
                             out=sb_h1s[sl][:], in_=h1_sp[i * 128:(i + 1) * 128, :]),
                         wait=[(sV, G.get(f"h1u{sl}", 0))], inc=qH[sl], incn=16)
                    h1v = qH[sl].n
                chb = sb_hf[:, i * QT:(i + 1) * QT]
                P.op("scalar",
                     lambda e, d=chb, sl=sl, i=i: e.activation(
                         d, sb_rsb[sl][:], AF.Identity,
                         bias=sb_bo[:, i:i + 1], scale=1.0),
                     wait=[(qRS[sl], rv), (qX[0], bov)],
                     inc=sS)
                G[f"rsb{sl}"] = evv = sS.n
                if which == 1:
                    P.op("vector",
                         lambda e, d=chb, i=i: e.tensor_add(
                             d, d, sb_xmT[:, i * QT:(i + 1) * QT]),
                         wait=[(sS, evv), (sV, xprep_done)], inc=sV)
                    chv = sV.n
                else:
                    P.op("vector",
                         lambda e, d=chb, s2=sb_h1s[sl]: e.tensor_add(
                             d, d, s2[:]),
                         wait=[(sS, evv), (qH[sl], h1v)], inc=sV)
                    G[f"h1u{sl}"] = chv = sV.n
                if which == 2:
                    P.op("scalar",
                         lambda e, d=chb, i=i, sp=spill: e.dma_start(
                             out=sp[i * 128:(i + 1) * 128, :], in_=d),
                         wait=[(sV, chv)], inc=qSp[sl], incn=16, dma=True)
                    G[f"chq{sl}"] = qSp[sl].n
                cbv, sqv = ln_stats_chunk(stp, i, chb, (sV, chv))
                G[f"chu{sl}"] = max(cbv, 0)

            ivv = ln_finish(stp)

            if which == 1:
                bcv = emit_bcasts(ps_bc, ivv)
                s0 = sS.n
                for i in range(ND):
                    sl = i % 2
                    chb = sb_hf[:, i * QT:(i + 1) * QT]
                    nv = norm_apply(chb, chb, bcv, extra_wait=[(sS, s0)])
                    P.op("scalar",
                         lambda e, d=chb, i=i: e.dma_start(
                             out=h1_sp[i * 128:(i + 1) * 128, :], in_=d),
                         wait=[(sV, nv)], inc=qSp[sl], incn=16, dma=True)
                    G[f"n1q{sl}"] = qSp[sl].n
                    cbv, sqv = ln_stats_chunk(stp, i, chb, (sV, nv))
                iv2 = ln_finish(stp)
                bcv = emit_bcasts(ps_bc, iv2)
                s0 = sS.n
                for i in range(ND):
                    sl = i % 2
                    chb = sb_ch[sl]
                    src_ap = sb_hf[:, i * QT:(i + 1) * QT]
                    nv = norm_apply(chb[:], src_ap, bcv,
                                    extra_wait=[(sS, s0),
                                                (sS, G.get(f"n2s{sl}", 0))])
                    G[f"n2u{sl}"] = nv
                    P.op("scalar",
                         lambda e, d=chb, i=i: e.activation(
                             sb_cb[:], d[:], AF.Identity,
                             bias=sb_ln1b[:, i:i + 1], scale=sb_ln1w[:, i:i + 1]),
                         wait=[(sV, nv), (qAG, G.get("ag2q", 0))],
                         inc=sS)
                    G[f"n2s{sl}"] = xv = sS.n
                    P.op("scalar",
                         lambda e, i=i: e.dma_start(
                             out=ag2_in[i * 128:(i + 1) * 128, :], in_=sb_cb[:]),
                         wait=[(sS, xv)], inc=qAG, incn=16, dma=True)
                    G["ag2q"] = qAG.n
                    if i % 4 == 3:
                        G[f"agsub{i // 4}"] = [(qAG, qAG.n)]
            else:
                bcv = emit_bcasts(ps_bc, ivv)
                s0 = sS.n
                G["x2q0"] = qSp[0].n
                G["x2q1"] = qSp[1].n
                for i in range(ND):
                    sl = i % 2
                    chb = sb_ch[sl]
                    src_ap = sb_hf[:, i * QT:(i + 1) * QT]
                    nv = norm_apply(chb[:], src_ap, bcv,
                                    extra_wait=[(sS, s0),
                                                (sS, G.get(f"x2s{sl}", 0)),
                                                (qSp[sl], G.get(f"x2q{sl}", 0))])
                    G[f"n3u{sl}"] = nv
                    P.op("scalar",
                         lambda e, d=chb, sl=sl, i=i: e.activation(
                             sb_ch[1 - sl][:], d[:], AF.Identity,
                             bias=sb_ln2b[:, i:i + 1], scale=sb_ln2w[:, i:i + 1]),
                         wait=[(sV, nv), (qSp[1 - sl], G.get(f"x2q{1 - sl}", 0)),
                               (sV, G.get(f"x2u{1 - sl}", 0))],
                         inc=sS)
                    G[f"x2s{1 - sl}"] = xv = sS.n
                    P.op("scalar",
                         lambda e, sl=sl, i=i: e.dma_start(
                             out=xg_sp[i * 128:(i + 1) * 128, :],
                             in_=sb_ch[1 - sl][:]),
                         wait=[(sS, xv)], inc=qSp[1 - sl], incn=16, dma=True)
                    G[f"x2q{1 - sl}"] = qSp[1 - sl].n
                    P.op("vector",
                         lambda e, sl=sl, i=i: e.tensor_copy(
                             G["sb_xln2T"][:, i * QT:(i + 1) * QT],
                             sb_ch[1 - sl][:]),
                         wait=[(sS, xv)], inc=sV)
                    G[f"x2u{1 - sl}"] = sV.n

        def gating(scope):
            barrier([("gxT", "sT"), ("gev", "sV"), ("cfmv", "sV"),
                     ("cbbv", "sV")])
            A = scope.enter_context
            sb_gt = A(nc.sbuf_tensor("sb_gt", [128, 32], F32))
            sb_gr = A(nc.sbuf_tensor("sb_gr", [128, 4], F32))
            sb_gm = A(nc.sbuf_tensor("sb_gm", [128, 4], F32))
            sb_gc = A(nc.sbuf_tensor("sb_gc", [128, 10], F32))
            sb_gx = A(nc.sbuf_tensor("sb_gx", [128, 128], F32))
            sb_cbf = A(nc.sbuf_tensor("sb_cbf", [128, 4], BF16))
            sb_cfm = A(nc.sbuf_tensor("sb_cfm", [4, QT], BF16))
            ps_g = A(nc.psum_tensor("ps_g", [128, 512], F32))
            ps_gt = A(nc.psum_tensor("ps_gt", [128, 1024], BF16))
            ps_cb = A(nc.psum_tensor("ps_cb", [128, 512], F32))
            assert nc.sbuf_bytes_remaining >= 0, "sbuf overflow gating"

            P.op("gpsimd",
                 lambda e: e.dma_start(out=G["sb_be2r"][0:1, :], in_=be2[:, :]),
                 inc=qW[0], incn=16)
            G["be2rq"] = qW[0].n

            for tt in range(8):
                for dc in range(ND):
                    P.op("sync",
                         lambda e, tt=tt, dc=dc: e.dma_start(
                             out=sb_gx[:],
                             in_=xg_sp[dc * 128:(dc + 1) * 128,
                                       tt * 128:(tt + 1) * 128]),
                         wait=[(sT, G.get("gxT", 0)), (qSp[0], qSp[0].n), (qSp[1], qSp[1].n)],
                         inc=qX[1], incn=16)
                    gxv = qX[1].n
                    P.op("tensor",
                         lambda e, dc=dc: e.matmul(
                             ps_g[:, 0:4], sb_gx[:], sb_wg[:, dc * 4:(dc + 1) * 4],
                             start=(dc == 0), stop=(dc == ND - 1)),
                         wait=[(qX[1], gxv)] +
                              ([(sV, G.get("gev", 0))] if dc == 0 else []),
                         inc=sT)
                    G["gxT"] = sT.n
                mmv = sT.n
                P.op("vector",
                     lambda e, tt=tt: e.tensor_copy(
                         sb_gt[:, tt * 4:(tt + 1) * 4], ps_g[:, 0:4]),
                     wait=[(sT, mmv)], inc=sV)
                G["gev"] = sV.n

            for tt in range(8):
                gsl = sb_gt[:, tt * 4:(tt + 1) * 4]
                P.op("vector", lambda e, g=gsl: e.reduce_max(
                    sb_gc[:, 0:1], g, axis=AX, negate=True), inc=sV)
                vv = sV.n
                P.op("scalar", lambda e, g=gsl: e.activation(
                    sb_gr[:], g, AF.Exp, bias=sb_gc[:, 0:1], scale=1.0,
                    accum_out=sb_gc[:, 1:2]),
                    wait=[(sV, vv)], inc=sS)
                ev = sS.n
                P.op("vector", lambda e: e.reciprocal(sb_gc[:, 2:3], sb_gc[:, 1:2]),
                     wait=[(sS, ev)], inc=sV)
                P.op("vector", lambda e: e.tensor_scalar_mul(
                    sb_gr[:], sb_gr[:], sb_gc[:, 2:3]), inc=sV)
                pairs = [(0, 1), (0, 2), (0, 3), (1, 2), (1, 3), (2, 3)]
                for k, (pa, pb_) in enumerate(pairs):
                    P.op("vector",
                         lambda e, pa=pa, pb_=pb_, k=k: e.tensor_tensor(
                             sb_gc[:, 3 + k:4 + k], sb_gr[:, pa:pa + 1],
                             sb_gr[:, pb_:pb_ + 1], ALU.is_ge), inc=sV)
                bb = [sb_gc[:, 3 + k:4 + k] for k in range(6)]
                cnt = sb_gc[:, 9:10]
                plans = [
                    ((bb[0], -1.0, 3.0), [(-1, bb[1]), (-1, bb[2])]),
                    ((bb[3], -1.0, 2.0), [(1, bb[0]), (-1, bb[4])]),
                    ((bb[5], -1.0, 1.0), [(1, bb[1]), (1, bb[3])]),
                    ((bb[2], 1.0, 0.0), [(1, bb[4]), (1, bb[5])]),
                ]
                for eidx, ((t0, m0, a0), rest) in enumerate(plans):
                    P.op("vector",
                         lambda e, t0=t0, m0=m0, a0=a0: e.tensor_scalar(
                             cnt, t0, m0, a0, ALU.mult, ALU.add), inc=sV)
                    for sgn, tap in rest:
                        op = ALU.add if sgn > 0 else ALU.subtract
                        P.op("vector",
                             lambda e, tap=tap, op=op: e.tensor_tensor(
                                 cnt, cnt, tap, op), inc=sV)
                    P.op("vector",
                         lambda e, eidx=eidx: e.tensor_single_scalar(
                             sb_gm[:, eidx:eidx + 1], cnt, 1.5, ALU.is_lt), inc=sV)
                vv = sV.n
                P.op("scalar", lambda e: e.activation(sb_gr[:], sb_gr[:], AF.Exp),
                     wait=[(sV, vv)], inc=sS)
                ev2 = sS.n
                P.op("vector", lambda e: e.tensor_mul(sb_gr[:], sb_gr[:], sb_gm[:]),
                     wait=[(sS, ev2)], inc=sV)
                P.op("vector", lambda e: e.reduce_sum(sb_gc[:, 0:1], sb_gr[:],
                                                      axis=AX), inc=sV)
                P.op("vector", lambda e: e.reciprocal(sb_gc[:, 1:2], sb_gc[:, 0:1]),
                     inc=sV)
                P.op("vector", lambda e: e.tensor_scalar_mul(
                    sb_gr[:], sb_gr[:], sb_gc[:, 1:2]), inc=sV)
                P.op("vector", lambda e: e.tensor_copy(sb_cbf[:], sb_gr[:]), inc=sV)
                cv = sV.n
                po = (tt % 2) * 128
                P.op("tensor",
                     lambda e, po=po: e.transpose(
                         ps_gt[0:4, po:po + 128], sb_cbf[:], sb_idb[:]),
                     wait=[(sV, cv), (sV, G.get("cfmv", 0))], inc=sT)
                tv = sT.n
                P.op("vector",
                     lambda e, tt=tt, po=po: e.tensor_copy(
                         sb_cfm[0:4, tt * 128:(tt + 1) * 128],
                         ps_gt[0:4, po:po + 128]),
                     wait=[(sT, tv)], inc=sV)
                G["cfmv"] = sV.n
            cfmv = sV.n
            P.op("sync",
                 lambda e: e.dma_start(out=G["sb_combr"][0:1, :], in_=sb_cfm[:, :]),
                 wait=[(sV, cfmv)], inc=qX[0], incn=16)
            crv = qX[0].n
            for ee in range(NE):
                for hh in range(2):
                    P.op("tensor",
                         lambda e, ee=ee, hh=hh: e.matmul(
                             ps_cb[:, 0:512], sb_onesrb[0:1, :],
                             G["sb_combr"][0:1, ee * QT + hh * 512:
                                           ee * QT + (hh + 1) * 512],
                             start=True, stop=True),
                         wait=[(qX[0], crv), (sV, G.get("cbbv", 0))], inc=sT)
                    tv = sT.n
                    P.op("vector",
                         lambda e, ee=ee, hh=hh: e.tensor_copy(
                             G["sb_combb"][:, ee * QT + hh * 512:
                                           ee * QT + (hh + 1) * 512],
                             ps_cb[:, 0:512]),
                         wait=[(sT, tv)], inc=sV)
                    G["cbbv"] = sV.n
            G["combv"] = sV.n

        def experts(scope, sb_xln2T, sb_heT, sb_m, sb_wc):
            barrier([("ex0", "sS"), ("ex1", "sS"), ("exv0", "sV"),
                     ("exv1", "sV"), ("heV", "sV")], skip_engs=("gpsimd",))
            acc = [scope.enter_context(nc.psum_tensor(f"ps_e{i}", [128, 512], F32))
                   for i in range(8)]
            wload = {}

            def load_quarter(e_i, li, q4):
                wsrc = we1 if li == 0 else we2
                sl = q4 % 2
                vals = []
                for dc in range(ND):
                    qq = qE[sl * 4 + dc % 4]
                    src = wsrc[e_i, dc * 128:(dc + 1) * 128,
                               q4 * 512:(q4 + 1) * 512]
                    dst = sb_wc[sl][:, dc * 512:(dc + 1) * 512]
                    P.op("gpsimd",
                         lambda e, d=dst, ss=src: e.dma_start(out=d, in_=ss),
                         wait=[(sT, G.get(f"mwq{sl}", 0)), (qq, qq.n)],
                         inc=qq, incn=16)
                    vals.append((qq, qq.n))
                wload[(e_i, li, q4)] = vals

            seqs = [(e_i, li, q4) for e_i in range(NE) for li in range(2)
                    for q4 in range(4)]
            load_quarter(*seqs[0])
            load_quarter(*seqs[1])

            for si, (e_i, li, q4) in enumerate(seqs):
                rhs_sb = sb_xln2T if li == 0 else sb_heT
                sl = q4 % 2
                vals = wload.pop((e_i, li, q4))
                for pp in range(2):
                    p_i = q4 * 2 + pp
                    st = p_i % 2
                    bs = acc[st * 4:(st + 1) * 4]
                    for dc in range(ND):
                        for tg in range(2):
                            for ii in range(2):
                                pb = bs[ii * 2 + tg]
                                w = []
                                if tg == 0 and ii == 0:
                                    w.append(vals[dc])
                                    if dc == 0:
                                        w += [(sS, G.get(f"ex{st}", 0)),
                                              (sV, G.get(f"exv{st}", 0)),
                                              (sV, G.get("combv", 0))]
                                        if li == 1:
                                            w.append((sV, G.get("heV", 0)))
                                P.op("tensor",
                                     lambda e, pb=pb, sl=sl, dc=dc, pp=pp,
                                     ii=ii, tg=tg, li=li, rs=rhs_sb: e.matmul(
                                         pb[:],
                                         sb_wc[sl][:, dc * 512 + pp * 256 +
                                                   ii * 128: dc * 512 +
                                                   pp * 256 + (ii + 1) * 128],
                                         rs[:, dc * QT + tg * 512:
                                            dc * QT + (tg + 1) * 512],
                                         start=(dc == 0),
                                         stop=(dc == ND - 1 and li == 0)),
                                     wait=w,
                                     inc=(sT if tg == 1 and ii == 1 else None))
                    if li == 1:
                        for tg in range(2):
                            for ii in range(2):
                                pb = bs[ii * 2 + tg]
                                col = p_i * 2 + ii
                                l_ap = G["sb_be2r"][0:1,
                                                    e_i * D + col * 128:
                                                    e_i * D + (col + 1) * 128]
                                r_ap = G["sb_combr"][0:1,
                                                     e_i * QT + tg * 512:
                                                     e_i * QT + (tg + 1) * 512]
                                P.op("tensor",
                                     lambda e, pb=pb, l=l_ap, r=r_ap: e.matmul(
                                         pb[:], l, r, start=False, stop=True),
                                     inc=(sT if tg == 1 and ii == 1 else None))
                    mmv = sT.n
                    if pp == 1:
                        G[f"mwq{sl}"] = mmv
                        if si + 2 < len(seqs):
                            load_quarter(*seqs[si + 2])
                    for tg in range(2):
                        for ii in range(2):
                            pb = bs[ii * 2 + tg]
                            col = p_i * 2 + ii
                            if li == 0:
                                dst = sb_heT[:, col * QT + tg * 512:
                                             col * QT + (tg + 1) * 512]
                                P.op("scalar",
                                     lambda e, pb=pb, d=dst, col=col, e_i=e_i:
                                     e.activation(
                                         d, pb[:], AF.Gelu,
                                         bias=sb_be1[:, e_i * ND + col:
                                                     e_i * ND + col + 1],
                                         scale=1.0),
                                     wait=[(sT, mmv)], inc=sS)
                                gv = sS.n
                                cb_ap = G["sb_combb"][
                                    :, e_i * QT + tg * 512:
                                    e_i * QT + (tg + 1) * 512]
                                P.op("vector",
                                     lambda e, d=dst, cb=cb_ap: e.tensor_mul(
                                         d, d, cb),
                                     wait=[(sS, gv)], inc=sV)
                            else:
                                dst = sb_m[:, col * QT + tg * 512:
                                           col * QT + (tg + 1) * 512]
                                if e_i == 0:
                                    P.op("vector",
                                         lambda e, pb=pb, d=dst: e.tensor_copy(
                                             d, pb[:]),
                                         wait=[(sT, mmv)], inc=sV)
                                else:
                                    P.op("vector",
                                         lambda e, pb=pb, d=dst: e.tensor_add(
                                             d, d, pb[:]),
                                         wait=[(sT, mmv)], inc=sV)
                    G[f"ex{st}"] = sS.n
                    G[f"exv{st}"] = sV.n
                if li == 0 and q4 == 3:
                    G["heV"] = sV.n

        def final_ln(scope, sb_m):
            barrier([("fs0", "sV"), ("fs1", "sV"), ("oev", "sV"),
                     ("statfree", "sV"), ("bcfree", "sV"), ("stT", "sT")])
            A = scope.enter_context
            sb_s = A(nc.sbuf_tensor("sb_s", [128, ND * QT], F32))
            sb_stg = A(nc.sbuf_tensor("sb_stg", [128, D], F32))
            sb_stg2 = A(nc.sbuf_tensor("sb_stg2", [128, D], F32))
            sb_ch = [A(nc.sbuf_tensor(f"sb_fch{i}", [128, QT], F32))
                     for i in range(2)]
            sb_h1s = [A(nc.sbuf_tensor(f"sb_fh1{i}", [128, QT], F32))
                      for i in range(2)]
            ps_x = A(nc.psum_tensor("pf_x", [128, 512], F32))
            ps_s = [A(nc.psum_tensor(f"pf_s{i}", [128, 512], F32)) for i in range(4)]
            ps_bc = A(nc.psum_tensor("pf_bc", [128, 512], F32))
            G["sb_bcm"] = [A(nc.sbuf_tensor(f"sb_bcmf{i}", [128, 512], F32))
                           for i in range(2)]
            G["sb_bci"] = [A(nc.sbuf_tensor(f"sb_bcif{i}", [128, 512], F32))
                           for i in range(2)]
            stp = tuple(ps_s)
            assert nc.sbuf_bytes_remaining >= 0, "sbuf overflow final"

            for i in range(ND):
                sl = i % 2
                ssl = sb_s[:, i * QT:(i + 1) * QT]
                P.op("sync",
                     lambda e, i=i, sl=sl: e.dma_start(
                         out=sb_h1s[sl][:], in_=h_sp[i * 128:(i + 1) * 128, :]),
                     wait=[(sV, G.get(f"fs{sl}", 0)), (qSp[0], qSp[0].n), (qSp[1], qSp[1].n)],
                     inc=qH[sl], incn=16)
                P.op("sync",
                     lambda e, i=i, sl=sl: e.dma_start(
                         out=sb_ch[sl][:], in_=h1_sp[i * 128:(i + 1) * 128, :]),
                     wait=[(sV, G.get(f"fs{sl}", 0))], inc=qH[sl], incn=16)
                h1v = qH[sl].n
                P.op("vector",
                     lambda e, d=ssl, sl=sl: e.tensor_add(
                         d, sb_h1s[sl][:], sb_ch[sl][:]),
                     wait=[(qH[sl], h1v)], inc=sV)
                P.op("vector",
                     lambda e, d=ssl, i=i, m=sb_m: e.tensor_add(
                         d, d, m[:, i * QT:(i + 1) * QT]), inc=sV)
                G[f"fs{sl}"] = svv = sV.n
                ln_stats_chunk(stp, i, ssl, (sV, svv))
            ivv = ln_finish(stp)
            bcv = emit_bcasts(ps_bc, ivv)
            for i in range(ND):
                ssl = sb_s[:, i * QT:(i + 1) * QT]
                norm_apply(ssl, ssl, bcv)
            nv = sV.n
            pxb = [ps_x, ps_s[0]]
            stg2 = [sb_stg, sb_stg2]
            oevk = {}
            for tt in range(8):
                sg = stg2[tt % 2]
                for i in range(ND):
                    k = tt * ND + i
                    pb = pxb[k % 2]
                    src = sb_s[:, i * QT + tt * 128: i * QT + (tt + 1) * 128]
                    P.op("tensor",
                         lambda e, pb=pb, src=src: e.transpose(
                             pb[:, 0:128], src, sb_idf[:]),
                         wait=[(sV, nv), (sV, oevk.get(k - 2, 0))], inc=sT)
                    tv = sT.n
                    P.op("vector",
                         lambda e, sg=sg, pb=pb, i=i: e.tensor_copy(
                             sg[:, i * 128:(i + 1) * 128], pb[:, 0:128]),
                         wait=[(sT, tv)] +
                              ([(qO, G.get(f"stgq{tt % 2}", 0))]
                               if i == 0 else []),
                         inc=sV)
                    oevk[k] = sV.n
                ovv = sV.n
                P.op("sync",
                     lambda e, tt=tt, sg=sg: e.dma_start(
                         out=out_mine[tt * 128:(tt + 1) * 128, :], in_=sg[:]),
                     wait=[(sV, ovv), (qO, qO.n)], inc=qO, incn=16, dma=True)
                G[f"stgq{tt % 2}"] = qO.n
            P.op("sync", None, wait=[(qO, qO.n)])
            P.op("gpsimd", None, wait=[(qO, qO.n)])

        with ExitStack() as sc1:
            A1 = sc1.enter_context
            sbQ = A1(nc.sbuf_tensor("sbQ1", [128, HPC * S], BF16))
            sbK = A1(nc.sbuf_tensor("sbK1", [128, HPC * S], BF16))
            sbV = A1(nc.sbuf_tensor("sbV1", [128, NT * HPC * 129], BF16))
            sbP1 = [A1(nc.sbuf_tensor(f"sbP1{i}", [128, S], BF16))
                    for i in range(2)]
            ps = {k: A1(nc.psum_tensor(f"a1{k}", [128, 512], F32))
                  for k in ("a", "b", "acc0", "acc1", "r")}
            ps["t0"] = A1(nc.psum_tensor("a1t0", [128, 1024], BF16))
            ps["t1"] = A1(nc.psum_tensor("a1t1", [128, 1024], BF16))
            ps["t2"] = A1(nc.psum_tensor("a1t2", [128, 1024], BF16))
            with ExitStack() as sq:
                qkv_proj(1, sbQ, sbK, sbV, ps, sq)
            sbAT = A1(nc.sbuf_tensor("sbAT1", [128, HPC * S], BF16))
            with ExitStack() as ssc:
                sbSC1 = [ssc.enter_context(
                    nc.sbuf_tensor(f"sbSC1{i}", [128, S], F32))
                    for i in range(2)]
                attention(1, sbQ, sbK, sbV, sbAT, sbP1, sbSC1, ps)
            with ExitStack() as sw:
                wo_partial(1, sbAT, ps, sw)
        collective("ReduceScatter", rs1_in, rs1_out, "cc_rs1",
                   subwaits=[G[f"wosub{g}"] for g in range(4)])
        with ExitStack() as sc2:
            res_ln(1, sc2)
        collective("AllGather", ag2_in, ag2_out, "cc_ag2",
                   subwaits=[G[f"agsub{g}"] for g in range(4)])
        with ExitStack() as sc3:
            A3 = sc3.enter_context
            sbQ = A3(nc.sbuf_tensor("sbQ2", [128, HPC * S], BF16))
            sbK = A3(nc.sbuf_tensor("sbK2", [128, HPC * S], BF16))
            sbV = A3(nc.sbuf_tensor("sbV2", [128, NT * HPC * 129], BF16))
            sbP2 = [A3(nc.sbuf_tensor(f"sbP2{i}", [128, S], BF16))
                    for i in range(2)]
            ps = {k: A3(nc.psum_tensor(f"a2{k}", [128, 512], F32))
                  for k in ("a", "b", "acc0", "acc1", "r")}
            ps["t0"] = A3(nc.psum_tensor("a2t0", [128, 1024], BF16))
            ps["t1"] = A3(nc.psum_tensor("a2t1", [128, 1024], BF16))
            ps["t2"] = A3(nc.psum_tensor("a2t2", [128, 1024], BF16))
            with ExitStack() as sq:
                qkv_proj(2, sbQ, sbK, sbV, ps, sq)
            sbAT = A3(nc.sbuf_tensor("sbAT2", [128, HPC * S], BF16))
            with ExitStack() as ssc:
                sbSC2 = [ssc.enter_context(
                    nc.sbuf_tensor(f"sbSC2{i}", [128, S], F32))
                    for i in range(2)]
                attention(2, sbQ, sbK, sbV, sbAT, sbP2, sbSC2, ps)
            with ExitStack() as sw:
                wo_partial(2, sbAT, ps, sw)
        collective("ReduceScatter", rs3_in, rs3_out, "cc_rs3",
                   subwaits=[G[f"wosub{g}"] for g in range(4)])
        with ExitStack() as sc4:
            A4 = sc4.enter_context
            sb_m = A4(nc.sbuf_tensor("sb_m", [128, ND * QT], BF16))
            with ExitStack() as scm:
                Am = scm.enter_context
                G["sb_xln2T"] = Am(nc.sbuf_tensor("sb_xln2T", [128, ND * QT], BF16))
                with ExitStack() as sc4b:
                    res_ln(2, sc4b)
                G["sb_be2r"] = Am(nc.sbuf_tensor("sb_be2r", [1, NE * D], BF16))
                G["sb_combr"] = Am(nc.sbuf_tensor("sb_combr", [1, NE * QT], BF16))
                G["sb_combb"] = Am(nc.sbuf_tensor("sb_combb", [128, NE * QT], BF16))
                sb_heT = Am(nc.sbuf_tensor("sb_heT", [128, ND * QT], BF16))
                sb_mwc = [Am(nc.sbuf_tensor(f"sb_mwc{i}", [128, ND * 512], BF16))
                          for i in range(2)]
                with ExitStack() as sc5:
                    gating(sc5)
                with ExitStack() as sc6:
                    experts(sc6, G["sb_xln2T"], sb_heT, sb_m, sb_mwc)
            with ExitStack() as sc7:
                final_ln(sc7, sb_m)

        if dbg is not None:
            P.op("sync",
                 lambda e: e.dma_start(out=dbg_out.ap(), in_=dsrc.ap()),
                 wait=[(sS, sS.n), (sV, sV.n), (qSp[0], qSp[0].n), (qSp[1], qSp[1].n), (qAG, qAG.n),
                       (cc, cc.n)],
                 inc=qO, incn=16)
            P.op("sync", None, wait=[(qO, qO.n)])

        block = E(nc.Block())

        def replay(name):
            def section(eng):
                hw = {}
                for fn, w, inc, incn in P.ops[name]:
                    for s_, v in w:
                        if hw.get(id(s_), -1) < v:
                            eng.wait_ge(s_.h, v)
                            hw[id(s_)] = v
                    if fn is not None:
                        r = fn(eng)
                        if inc is not None:
                            r.then_inc(inc.h, incn)
            return section

        block.sync(replay("sync"))
        block.gpsimd(replay("gpsimd"))
        block.vector(replay("vector"))
        block.scalar(replay("scalar"))
        block.tensor(replay("tensor"))

    return nc



def _col(v, nch):
    return np.ascontiguousarray(np.asarray(v, dtype=np.float32)
                                .reshape(nch, 128).T)


def make_in_maps(inputs):
    ii = {k: np.asarray(v) for k, v in inputs.items()}
    x = ii["L0_attn_x"].astype(np.float32, copy=False)
    wqkv = ii["L0_attn_w_qkv"].astype(np.float32, copy=False)
    bqkv = ii["L0_attn_b_qkv"].astype(np.float32, copy=False)
    w_o = ii["L0_attn_w_o"].astype(np.float32, copy=False)
    b_o = ii["L0_attn_b_o"].astype(np.float32, copy=False)
    wq2 = ii["L0_ffn_wq"].astype(np.float32, copy=False)
    wk2 = ii["L0_ffn_wk"].astype(np.float32, copy=False)
    wv2 = ii["L0_ffn_wv"].astype(np.float32, copy=False)
    wo2 = ii["L0_ffn_wo"].astype(np.float32, copy=False)
    bq2, bk2, bv2 = (ii["L0_ffn_bq"], ii["L0_ffn_bk"], ii["L0_ffn_bv"])
    maps = []
    for c in range(8):
        b, p = c // 4, c % 4
        heads = list(range(4 * p, 4 * p + 4))
        w1 = np.concatenate([wqkv[:, 384 * h:384 * (h + 1)] for h in heads], axis=1)
        b1 = np.concatenate([bqkv[384 * h:384 * (h + 1)] for h in heads])
        w2cols, b2cols = [], []
        for h in heads:
            sl = slice(128 * h, 128 * (h + 1))
            w2cols += [wq2[:, sl], wk2[:, sl], wv2[:, sl]]
            b2cols += [np.asarray(bq2)[sl], np.asarray(bk2)[sl],
                       np.asarray(bv2)[sl]]
        w2 = np.concatenate(w2cols, axis=1)
        b2 = np.concatenate(b2cols)
        m = {
            "x_full": np.ascontiguousarray(x[b]),
            "x_mine": np.ascontiguousarray(x[b, QT * p:QT * (p + 1)]),
            "wqkv1": np.ascontiguousarray(w1),
            "bqkv1": _col(b1, NCH),
            "wom1": np.ascontiguousarray(w_o[512 * p:512 * (p + 1)]),
            "bo1": _col(b_o, ND),
            "wqkv2": np.ascontiguousarray(w2),
            "bqkv2": _col(b2, NCH),
            "wom2": np.ascontiguousarray(wo2[512 * p:512 * (p + 1)]),
            "bo2": _col(ii["L0_ffn_bo"], ND),
            "ln1w": _col(ii["L0_ffn_ln1_w"], ND),
            "ln1b": _col(ii["L0_ffn_ln1_b"], ND),
            "ln2w": _col(ii["L0_ffn_ln2_w"], ND),
            "ln2b": _col(ii["L0_ffn_ln2_b"], ND),
            "wgate": np.ascontiguousarray(
                ii["L0_ffn_w_gate"].astype(np.float32, copy=False)
                .reshape(ND, 128, NE).transpose(1, 0, 2).reshape(128, ND * NE)),
            "we1": np.ascontiguousarray(ii["L0_ffn_we1"].astype(np.float32,
                                                               copy=False)),
            "be1": np.ascontiguousarray(
                ii["L0_ffn_be1"].astype(np.float32, copy=False)
                .reshape(NE, ND, 128).transpose(2, 0, 1).reshape(128, NE * ND)),
            "we2": np.ascontiguousarray(ii["L0_ffn_we2"].astype(np.float32,
                                                               copy=False)),
            "be2": np.ascontiguousarray(
                ii["L0_ffn_be2"].astype(np.float32, copy=False).reshape(64, 128)),
        }
        maps.append(m)
    return maps


def kernel(**inputs):
    maps = make_in_maps(inputs)
    nc = build_nc()
    res = run_bass_kernel_spmd(nc, maps, core_ids=list(range(8)))
    out = np.empty((B, S, D), dtype=np.float32)
    for c in range(8):
        b, p = c // 4, c % 4
        out[b, QT * p:QT * (p + 1)] = res.results[c]["out_mine"]
    return out


# version 14 (final)
# speedup vs baseline: 1.1590x; 1.0133x over previous
import numpy as np
import ml_dtypes
from contextlib import ExitStack

import concourse.bass as bass
import concourse.mybir as mybir
from concourse.bass_utils import run_bass_kernel_spmd

F32 = mybir.dt.float32
BF16 = mybir.dt.bfloat16
AF = mybir.ActivationFunctionType
ALU = mybir.AluOpType
AX = mybir.AxisListType.X
_BF = ml_dtypes.bfloat16

B, S, D = 2, 4096, 2048
HD = 128
HPC = 4
QT = 1024
ND = D // 128
NT = S // 128
NRG = 4
NCH = HPC * 3
NE = 4
INV_SQRT_HD = 1.0 / np.sqrt(np.float64(HD))
MASK1 = np.float32(-9e15)
MASK2 = np.finfo(np.float32).min
EPS = 1e-5
INV_D = 1.0 / D
GROUPS = [[0, 1, 2, 3], [4, 5, 6, 7]]

ENGS = ("sync", "gpsimd", "vector", "scalar", "tensor")

STRICT = False
RELAX_ENGS = ("tensor", "scalar")


def _rope_tables():
    inv_freq = 1.0 / 10000.0 ** (np.arange(0, HD, 2, dtype=np.float64) / HD)
    t = np.arange(S, dtype=np.float64)
    emb = np.concatenate([np.outer(inv_freq, t)] * 2, axis=0)
    cos_e, sin_e = np.cos(emb), np.sin(emb)
    sgn = np.ones((HD, 1), dtype=np.float64)
    sgn[: HD // 2] = -1.0
    out = {}
    for a, (cu, su) in {1: (sin_e, cos_e), 2: (cos_e, sin_e)}.items():
        out[(a, "cq")] = np.ascontiguousarray((cu * INV_SQRT_HD).astype(np.float32))
        out[(a, "sq")] = np.ascontiguousarray((su * sgn * INV_SQRT_HD).astype(np.float32))
        out[(a, "ck")] = np.ascontiguousarray(cu.astype(np.float32))
        out[(a, "sk")] = np.ascontiguousarray((su * sgn).astype(np.float32))
    return out


def _rot_matrix():
    R = np.zeros((128, 128), dtype=np.float64)
    for i in range(64):
        R[i, i + 64] = 1.0
        R[i + 64, i] = 1.0
    return np.ascontiguousarray(R.T.astype(_BF))


def _causal_mask(val):
    m = np.zeros((128, 128), dtype=np.float32)
    m[np.triu_indices(128, k=1)] = val
    return m


class _Sem:
    __slots__ = ("name", "n", "h", "owner")

    def __init__(self, name, owner=None):
        self.name, self.n, self.h, self.owner = name, 0, None, owner


class _Plan:
    def __init__(self):
        self.ops = {e: [] for e in ENGS}
        self.own = {}

    def op(self, eng, fn=None, wait=(), inc=None, incn=1, dma=False):
        if STRICT:
            w = tuple((s, v) for (s, v) in wait if v > 0)
            if fn is not None and eng in self.own:
                o = self.own[eng]
                if o.n > 0:
                    w = ((o, o.n),) + w
        else:
            if eng in RELAX_ENGS:
                w = tuple((s, v) for (s, v) in wait
                          if v > 0 and s is not self.own.get(eng))
                if dma and fn is not None and eng in self.own:
                    o = self.own[eng]
                    if o.n > 0:
                        w = ((o, o.n),) + w
            else:
                w = tuple((s, v) for (s, v) in wait if v > 0)
                if fn is not None and eng in self.own:
                    o = self.own[eng]
                    if o.n > 0:
                        w = ((o, o.n),) + w
        if fn is None and inc is not None:
            raise RuntimeError("inc without instruction")
        if fn is not None or w:
            self.ops[eng].append((fn, w, inc, incn))
        if inc is not None:
            inc.n += incn
        return inc.n if inc is not None else None


def build_nc(dbg=None):
    nc = bass.Bass("TRN2", target_bir_lowering=False,
                   detect_race_conditions=STRICT)

    x_full = nc.dram_tensor("x_full", [S, D], F32, kind="ExternalInput")
    x_mine = nc.dram_tensor("x_mine", [QT, D], F32, kind="ExternalInput")
    wqkv1 = nc.dram_tensor("wqkv1", [D, NCH * 128], F32, kind="ExternalInput")
    bqkv1 = nc.dram_tensor("bqkv1", [128, NCH], F32, kind="ExternalInput")
    wom1 = nc.dram_tensor("wom1", [HPC * 128, D], F32, kind="ExternalInput")
    bo1 = nc.dram_tensor("bo1", [128, ND], F32, kind="ExternalInput")
    wqkv2 = nc.dram_tensor("wqkv2", [D, NCH * 128], F32, kind="ExternalInput")
    bqkv2 = nc.dram_tensor("bqkv2", [128, NCH], F32, kind="ExternalInput")
    wom2 = nc.dram_tensor("wom2", [HPC * 128, D], F32, kind="ExternalInput")
    bo2 = nc.dram_tensor("bo2", [128, ND], F32, kind="ExternalInput")
    ln1w = nc.dram_tensor("ln1w", [128, ND], F32, kind="ExternalInput")
    ln1b = nc.dram_tensor("ln1b", [128, ND], F32, kind="ExternalInput")
    ln2w = nc.dram_tensor("ln2w", [128, ND], F32, kind="ExternalInput")
    ln2b = nc.dram_tensor("ln2b", [128, ND], F32, kind="ExternalInput")
    wgate = nc.dram_tensor("wgate", [128, ND * NE], F32, kind="ExternalInput")
    we1 = nc.dram_tensor("we1", [NE, D, D], F32, kind="ExternalInput")
    be1 = nc.dram_tensor("be1", [128, NE * ND], F32, kind="ExternalInput")
    we2 = nc.dram_tensor("we2", [NE, D, D], F32, kind="ExternalInput")
    be2 = nc.dram_tensor("be2", [64, 128], F32, kind="ExternalInput")
    out_mine = nc.dram_tensor("out_mine", [QT, D], F32, kind="ExternalOutput")

    rs1_in = nc.dram_tensor("rs1_in", [4, 4, 512, QT], BF16)
    rs1_out = nc.dram_tensor("rs1_out", [D, QT], BF16)
    ag2_in = nc.dram_tensor("ag2_in", [D, QT], BF16)
    ag2_out = nc.dram_tensor("ag2_out", [4, 4, 512, QT], BF16)
    rs3_in = nc.dram_tensor("rs3_in", [4, 4, 512, QT], BF16)
    rs3_out = nc.dram_tensor("rs3_out", [D, QT], BF16)
    h1_sp = nc.dram_tensor("h1_sp", [D, QT], F32)
    h_sp = nc.dram_tensor("h_sp", [D, QT], F32)
    xg_sp = nc.dram_tensor("xg_sp", [D, QT], F32)

    dbg_out = None
    dsrc = None
    if dbg is not None:
        dbg_srcs = {"rs1": rs1_out, "h1": h1_sp, "xln1": ag2_in, "rs3": rs3_out,
                    "h": h_sp, "xg": xg_sp}
        dsrc = dbg_srcs[dbg]
        dbg_out = nc.dram_tensor("dbg_out", list(dsrc.shape), dsrc.dtype,
                                 kind="ExternalOutput")

    tabs = _rope_tables()
    c_tab = {k: nc.inline_tensor(v, name=f"tab{k[0]}{k[1]}") for k, v in tabs.items()}
    c_idb = nc.inline_tensor(np.eye(128, dtype=_BF), name="ident_bf")
    c_idf = nc.inline_tensor(np.eye(128, dtype=np.float32), name="ident_f32")
    c_rot = nc.inline_tensor(_rot_matrix(), name="rotT")
    c_m1 = nc.inline_tensor(_causal_mask(MASK1), name="mask1")
    c_m2 = nc.inline_tensor(_causal_mask(MASK2), name="mask2")
    c_onesc = nc.inline_tensor(np.ones((128, 1), dtype=_BF), name="ones_col")
    c_onesrf = nc.inline_tensor(np.ones((1, 128), dtype=np.float32), name="ones_rf")
    c_onesrb = nc.inline_tensor(np.ones((1, 128), dtype=_BF), name="ones_rb")
    c_vones = nc.inline_tensor(np.ones((128, NT * HPC), dtype=_BF),
                               name="v_ones")

    with ExitStack() as top:
        E = top.enter_context

        def sem(name, owner=None):
            s = _Sem(name, owner)
            s.h = E(nc.semaphore(name))
            return s

        sT = sem("sT", "tensor")
        sV = sem("sV", "vector")
        sS = sem("sS", "scalar")
        qW = [sem(f"qW{i}") for i in range(2)]
        qX = [sem(f"qX{i}") for i in range(2)]
        qR = sem("qR")
        qSp = [sem(f"qS{i}") for i in range(2)]
        qAG = sem("qAG")
        qGs = [sem(f"qAGs{i}") for i in range(2)]
        qRS = [sem(f"qRS{i}") for i in range(2)]
        qE = [sem(f"qE{i}") for i in range(8)]
        qXT = [sem(f"qXT{i}") for i in range(2)]
        qO = sem("qO")
        cc = sem("cc")
        qH = [sem(f"qH{i}") for i in range(2)]
        SEM_OF = {"sT": sT, "sV": sV, "sS": sS, "qS0": qSp[0], "qS1": qSp[1], "qAG": qAG, "qO": qO}

        sb_idb = E(nc.sbuf_tensor("sb_idb", [128, 128], BF16))
        sb_idf = E(nc.sbuf_tensor("sb_idf", [128, 128], F32))
        sb_rot = E(nc.sbuf_tensor("sb_rot", [128, 128], BF16))
        sb_m1 = E(nc.sbuf_tensor("sb_m1", [128, 128], F32))
        sb_m2 = E(nc.sbuf_tensor("sb_m2", [128, 128], F32))
        sb_onesc = E(nc.sbuf_tensor("sb_onesc", [128, 1], BF16))
        sb_onesrf = E(nc.sbuf_tensor("sb_onesrf", [1, 128], F32))
        sb_onesrb = E(nc.sbuf_tensor("sb_onesrb", [1, 128], BF16))
        sb_vo = E(nc.sbuf_tensor("sb_vo", [128, NT * HPC], BF16))
        sb_bqkv = E(nc.sbuf_tensor("sb_bqkv", [128, NCH], F32))
        sb_bo = E(nc.sbuf_tensor("sb_bo", [128, ND], F32))
        sb_ln1w = E(nc.sbuf_tensor("sb_ln1w", [128, ND], F32))
        sb_ln1b = E(nc.sbuf_tensor("sb_ln1b", [128, ND], F32))
        sb_ln2w = E(nc.sbuf_tensor("sb_ln2w", [128, ND], F32))
        sb_ln2b = E(nc.sbuf_tensor("sb_ln2b", [128, ND], F32))
        sb_wg = E(nc.sbuf_tensor("sb_wg", [128, ND * NE], F32))
        sb_be1 = E(nc.sbuf_tensor("sb_be1", [128, NE * ND], F32))
        sb_mu = E(nc.sbuf_tensor("sb_mu", [1, QT], F32))
        sb_iv = E(nc.sbuf_tensor("sb_iv", [1, QT], F32))
        sb_rw = E(nc.sbuf_tensor("sb_rw", [1, QT], F32))
        sb_cb = E(nc.sbuf_tensor("sb_cb", [128, QT], BF16))
        sb_sq = E(nc.sbuf_tensor("sb_sq", [128, QT], BF16))
        sb_mx2 = [E(nc.sbuf_tensor(f"sb_mx2{i}", [128, 8], F32)) for i in range(2)]
        sb_nm2 = [E(nc.sbuf_tensor(f"sb_nm2{i}", [128, 1], F32)) for i in range(2)]
        sb_sm2 = [E(nc.sbuf_tensor(f"sb_sm2{i}", [128, 8], F32)) for i in range(2)]
        sb_sum = E(nc.sbuf_tensor("sb_sum", [128, 1], F32))
        sb_rcp = E(nc.sbuf_tensor("sb_rcp", [128, 1], F32))
        sb_pt = [E(nc.sbuf_tensor(f"sb_pt{i}", [128, 512], BF16)) for i in range(2)]
        sb_qf = E(nc.sbuf_tensor("sb_qf", [128, 512], F32))
        sb_qb = E(nc.sbuf_tensor("sb_qb", [128, 512], BF16))
        sb_t1 = E(nc.sbuf_tensor("sb_t1", [128, 512], F32))
        sb_t2 = E(nc.sbuf_tensor("sb_t2", [128, 512], F32))
        sb_at = E(nc.sbuf_tensor("sb_at", [128, 128], BF16))

        P = _Plan()
        P.own = {"tensor": sT, "vector": sV, "scalar": sS}
        G = {}

        def barrier(keys, skip_engs=(), nocc=False):
            for key, semname in keys:
                G[key] = SEM_OF[semname].n
    
            fence = [(x, x.n) for x in
                     (sT, sV, sS, qW[0], qW[1], qX[0], qX[1], qR, qSp[0],
                      qSp[1], qAG, qGs[0], qGs[1], qRS[0], qRS[1], qO,
                      qH[0], qH[1], *qE, qXT[0], qXT[1])]
            if not nocc:
                fence.append((cc, cc.n))
            for e2 in ENGS:
                if e2 not in skip_engs:
                    P.op(e2, None, wait=fence)

        for dst, src in [
            (sb_idb, c_idb.ap()), (sb_idf, c_idf.ap()), (sb_rot, c_rot.ap()),
            (sb_m1, c_m1.ap()), (sb_m2, c_m2.ap()),
            (sb_onesc, c_onesc.ap()), (sb_onesrf, c_onesrf.ap()),
            (sb_onesrb, c_onesrb.ap()),
            (sb_bo, bo1[:, :]), (sb_ln1w, ln1w[:, :]), (sb_ln1b, ln1b[:, :]),
            (sb_ln2w, ln2w[:, :]), (sb_ln2b, ln2b[:, :]),
            (sb_wg, wgate[:, :]), (sb_be1, be1[:, :]),
            (sb_bqkv, bqkv1[:, :]), (sb_vo, c_vones.ap()),
        ]:
            P.op("sync", lambda e, d=dst, s=src: e.dma_start(out=d[:], in_=s),
                 inc=qX[0], incn=16)
        G["consts"] = qX[0].n
        for _e in ENGS:
            P.op(_e, None, wait=[(qX[0], G["consts"])])

        def ln_stats_chunk(stp, i, src_ap, depv):
            (su0, su1, sq0, sq1) = stp
            P.op("vector", lambda e, d=src_ap: e.tensor_copy(sb_cb[:], d),
                 wait=[depv, (sT, G.get("stT", 0))], inc=sV)
            cbv = sV.n
            P.op("scalar", lambda e, d=src_ap: e.activation(sb_sq[:], d, AF.Square),
                 wait=[depv, (sT, G.get("stT", 0))], inc=sS)
            sqv = sS.n
            first, last = (i == 0), (i == ND - 1)
            for hh, (psu, psq) in enumerate(((su0, sq0), (su1, sq1))):
                w = [(sV, cbv), (sS, sqv)]
                if first and hh == 0:
                    w.append((sV, G.get("statfree", 0)))
                P.op("tensor",
                     lambda e, hh=hh, p=psu, st=first, sp=last: e.matmul(
                         p[0:1, 0:512], sb_onesc[:, 0:1],
                         sb_cb[:, hh * 512:(hh + 1) * 512], start=st, stop=sp),
                     wait=(w if hh == 0 else ()), inc=None)
                P.op("tensor",
                     lambda e, hh=hh, p=psq, st=first, sp=last: e.matmul(
                         p[0:1, 0:512], sb_onesc[:, 0:1],
                         sb_sq[:, hh * 512:(hh + 1) * 512], start=st, stop=sp),
                     inc=(sT if hh == 1 else None))
            G["stT"] = sT.n
            return cbv, sqv

        def ln_finish(stp):
            (su0, su1, sq0, sq1) = stp
            mmv = sT.n
            for hh, psu in ((0, su0), (1, su1)):
                P.op("vector",
                     lambda e, hh=hh, p=psu: e.tensor_scalar_mul(
                         sb_mu[0:1, hh * 512:(hh + 1) * 512], p[0:1, 0:512], INV_D),
                     wait=[(sT, mmv)], inc=sV)
            for hh, psq in ((0, sq0), (1, sq1)):
                P.op("vector",
                     lambda e, hh=hh, p=psq: e.tensor_scalar_mul(
                         sb_rw[0:1, hh * 512:(hh + 1) * 512], p[0:1, 0:512], INV_D),
                     wait=[(sT, mmv)], inc=sV)
            P.op("vector", lambda e: e.tensor_mul(sb_iv[0:1, :], sb_mu[0:1, :],
                                                  sb_mu[0:1, :]), inc=sV)
            P.op("vector", lambda e: e.tensor_sub(sb_rw[0:1, :], sb_rw[0:1, :],
                                                  sb_iv[0:1, :]), inc=sV)
            P.op("vector", lambda e: e.tensor_scalar_add(sb_rw[0:1, :],
                                                         sb_rw[0:1, :], EPS),
                 inc=sV)
            G["statfree"] = vv = sV.n
            P.op("scalar", lambda e: e.activation(sb_iv[0:1, :], sb_rw[0:1, :],
                                                  AF.Ln),
                 wait=[(sV, vv)], inc=sS)
            P.op("scalar", lambda e: e.activation(sb_iv[0:1, :], sb_iv[0:1, :],
                                                  AF.Exp, scale=-0.5), inc=sS)
            return sS.n

        def emit_bcasts(ps_bc, ivv):
            for hh in range(2):
                P.op("tensor",
                     lambda e, hh=hh, p=ps_bc: e.matmul(
                         p[:, 0:512], sb_onesrf[0:1, :],
                         sb_mu[0:1, hh * 512:(hh + 1) * 512], start=True, stop=True),
                     wait=[(sS, ivv), (sV, G.get("bcfree", 0)), (sV, sV.n)],
                     inc=sT)
                tv = sT.n
                P.op("vector",
                     lambda e, hh=hh, p=ps_bc, d=G["sb_bcm"][hh]: e.tensor_copy(
                         d[:], p[:, 0:512]),
                     wait=[(sT, tv)], inc=sV)
                P.op("tensor",
                     lambda e, hh=hh, p=ps_bc: e.matmul(
                         p[:, 0:512], sb_onesrf[0:1, :],
                         sb_iv[0:1, hh * 512:(hh + 1) * 512], start=True, stop=True),
                     wait=[(sV, sV.n)], inc=sT)
                tv = sT.n
                P.op("vector",
                     lambda e, hh=hh, p=ps_bc, d=G["sb_bci"][hh]: e.tensor_copy(
                         d[:], p[:, 0:512]),
                     wait=[(sT, tv)], inc=sV)
            return sV.n

        def norm_apply(dst_ap, src_ap, bcv, extra_wait=()):
            for hh in range(2):
                P.op("vector",
                     lambda e, hh=hh, d=dst_ap, s=src_ap, bm=G["sb_bcm"][hh]:
                     e.tensor_sub(
                         d[:, hh * 512:(hh + 1) * 512],
                         s[:, hh * 512:(hh + 1) * 512], bm[:]),
                     wait=[(sV, bcv)] + list(extra_wait), inc=sV)
                P.op("vector",
                     lambda e, hh=hh, d=dst_ap, bi=G["sb_bci"][hh]: e.tensor_mul(
                         d[:, hh * 512:(hh + 1) * 512],
                         d[:, hh * 512:(hh + 1) * 512], bi[:]),
                     inc=sV)
            G["bcfree"] = sV.n
            return sV.n

        def qkv_proj(a, sbQ, sbK, sbV, ps, scope):
            barrier([("pb0", "sS"), ("pb1", "sS"), ("pbv0", "sV"), ("pbv1", "sV"),
                     ("vtT", "sT"), ("vt0", "sV"), ("vt1", "sV"),
                     ("xtp0", "sV"), ("xtp1", "sV"), ("rotb", "sV"),
                     ("ropeb", "sV"), ("qbT", "sT"), ("ropedone", "sV"),
                     ("xbf_done", "sV"), ("xTfree", "sT")], nocc=True)
            vod = sbV[:].rearrange("p (x c) -> p x c", c=129)[
                :, :, 128:129].rearrange("p x c -> p (x c)")
            P.op("vector",
                 lambda e, d=vod: e.tensor_copy(d, sb_vo[:]),
                 wait=[(qX[0], G["consts"]), (sV, sV.n)], inc=sV)
            G["vones"] = sV.n
            A = scope.enter_context
            wsrc = wqkv1 if a == 1 else wqkv2
            if a == 2:
                P.op("sync", lambda e: e.dma_start(out=sb_bqkv[:], in_=bqkv2[:, :]),
                     wait=[(sS, sS.n)], inc=qX[0], incn=16)
            biasv = qX[0].n

            sb_tab = A(nc.sbuf_tensor(f"sb_tab{a}", [128, 4096], F32))
            sb_wc = [A(nc.sbuf_tensor(f"sb_wc{a}{i}", [128, ND * 128], BF16))
                     for i in range(2)]
            if a == 1:
                sb_xtm = [A(nc.sbuf_tensor(f"sb_xtm{i}", [128, D], BF16))
                          for i in range(2)]
            sb_xT = A(nc.sbuf_tensor(f"sb_xT{a}", [128, ND * 1024], BF16))
            assert nc.sbuf_bytes_remaining >= 0, "sbuf overflow qkv"

            wready = {}
            wguard = {0: 0, 1: 0}

            def load_wchunk(c):
                sl = c % 2
                for g in range(4):
                    src = wsrc[g * 512:(g + 1) * 512, c * 128:(c + 1) * 128]
                    src = src.rearrange("(dc p) j -> p dc j", p=128)
                    dst = sb_wc[sl][:].rearrange("p (dc j) -> p dc j", j=128)[
                        :, g * 4:(g + 1) * 4]
                    P.op("gpsimd",
                         lambda e, d=dst, ss=src: e.dma_start(out=d, in_=ss),
                         wait=([(sT, wguard[sl])] if g == 0 else ()),
                         inc=qW[sl], incn=16)
                wready[c] = qW[sl].n

            def load_tables(r):
                for i, k in enumerate(("cq", "sq", "ck", "sk")):
                    src = c_tab[(a, k)][:, r * 1024:(r + 1) * 1024]
                    dst = sb_tab[:, i * 1024:(i + 1) * 1024]
                    P.op("sync",
                         lambda e, d=dst, ss=src: e.dma_start(out=d, in_=ss),
                         wait=([(sV, G.get("ropedone", 0))] if i == 0 else ()),
                         inc=qR, incn=16)
                return qR.n

            xready = {}

            def load_rhs(r):
                if a == 2:
                    xr = []
                    for g in range(4):
                        src = ag2_out[g, r, :, :].rearrange(
                            "(dc p) t -> p dc t", p=128)
                        dst = sb_xT[:].rearrange("p (dc t) -> p dc t", t=1024)[
                            :, g * 4:(g + 1) * 4, :]
                        P.op("sync",
                             lambda e, d=dst, ss=src: e.dma_start(out=d, in_=ss),
                             wait=[(sT, G.get("xTfree", 0)),
                                   (cc, G["cc_ag2"][g]),
                                   (qX[1], qX[1].n)],
                             inc=qX[1], incn=16)
                        xr.append((qX[1], qX[1].n))
                    xready[r] = xr
                    return
                done = 0
                for tt in range(8):
                    row = r * 1024 + tt * 128
                    xsl = tt % 2
                    P.op("gpsimd",
                         lambda e, row=row, xsl=xsl: e.dma_start(
                             out=sb_xtm[xsl][:], in_=x_full[row:row + 128, :]),
                         wait=[(sT, G.get(f"xtm{xsl}", 0))],
                         inc=qXT[xsl], incn=16)
                    ldv = qXT[xsl].n
                    for bat in range(2):
                        pt = ps["t0"] if bat == 0 else ps["t1"]
                        for j in range(8):
                            dc = bat * 8 + j
                            P.op("tensor",
                                 lambda e, dc=dc, pt=pt, j=j, xsl=xsl: e.transpose(
                                     pt[:, j * 128:(j + 1) * 128],
                                     sb_xtm[xsl][:, dc * 128:(dc + 1) * 128],
                                     sb_idb[:]),
                                 wait=([(qXT[xsl], ldv),
                                        (sV, G.get(f"xtp{bat}", 0)),
                                        (sV, G.get(f"vt{bat}", 0)),
                                        (sT, G.get("xTfree", 0))] if j == 0 else ()),
                                 inc=(sT if j == 7 else None))
                        tpv = sT.n
                        G[f"xtm{xsl}"] = tpv
                        dst = sb_xT[:].rearrange("p (dc t) -> p dc t", t=1024)[
                            :, bat * 8:(bat + 1) * 8, tt * 128:(tt + 1) * 128]
                        src2 = pt[:].rearrange("p (dc j) -> p dc j", j=128)
                        P.op("vector",
                             lambda e, d=dst, ss=src2: e.tensor_copy(d, ss),
                             wait=[(sT, tpv)], inc=sV)
                        G[f"xtp{bat}"] = done = sV.n
                xready[r] = (sV, done)

            for r in range(NRG):
                G["xTfree"] = sT.n
                load_rhs(r)
                tabv = load_tables(r)
                wguard[0] = wguard[1] = sT.n
                load_wchunk(0)
                load_wchunk(1)
                for c in range(NCH):
                    head, kindi = c // 3, c % 3
                    for hf in range(2):
                        par = hf
                        pb = ps["a"] if par == 0 else ps["b"]
                        for dc in range(ND):
                            P.op("tensor",
                                 lambda e, pb=pb, sl=c % 2, dc=dc, hf=hf: e.matmul(
                                     pb[:],
                                     sb_wc[sl][:, dc * 128:(dc + 1) * 128],
                                     sb_xT[:, dc * 1024 + hf * 512:
                                           dc * 1024 + (hf + 1) * 512],
                                     start=(dc == 0), stop=(dc == ND - 1)),
                                 wait=(([(qW[c % 2], wready[c]),
                                         (xready[r][dc // 4] if a == 2
                                          else xready[r]),
                                         (sS, G.get(f"pb{par}", 0)),
                                         (sV, G.get(f"pbv{par}", 0)),
                                         (qX[0], biasv)]) if dc == 0 else
                                       ([xready[r][dc // 4]]
                                        if a == 2 and dc % 4 == 0 else ())),
                                 inc=(sT if dc == ND - 1 else None))
                        mmv = sT.n
                        if hf == 1 and c + 2 < NCH:
                            wguard[c % 2] = mmv
                            load_wchunk(c + 2)
                        tok0 = r * 1024 + hf * 512
                        if kindi < 2:
                            dst = (sbQ if kindi == 0 else sbK)[
                                :, head * S + tok0: head * S + tok0 + 512]
                            ti = 0 if kindi == 0 else 2
                            cos_ap = sb_tab[:, ti * 1024 + hf * 512:
                                            ti * 1024 + (hf + 1) * 512]
                            sin_ap = sb_tab[:, (ti + 1) * 1024 + hf * 512:
                                            (ti + 1) * 1024 + (hf + 1) * 512]
                            P.op("scalar",
                                 lambda e, p=pb, c=c: e.activation(
                                     sb_qf[:], p[:], AF.Identity,
                                     bias=sb_bqkv[:, c:c + 1], scale=1.0),
                                 wait=[(sT, mmv), (sV, G.get("ropeb", 0))], inc=sS)
                            G[f"pb{par}"] = ev = sS.n
                            P.op("vector",
                                 lambda e: e.tensor_copy(sb_qb[:], sb_qf[:]),
                                 wait=[(sS, ev), (qR, tabv),
                                       (sT, G.get("vtT", 0))], inc=sV)
                            cv = sV.n
                            P.op("tensor",
                                 lambda e: e.matmul(ps["r"][:], sb_rot[:], sb_qb[:],
                                                    start=True, stop=True),
                                 wait=[(sV, cv), (sV, G.get("rotb", 0))], inc=sT)
                            G["qbT"] = rv = sT.n
                            P.op("vector",
                                 lambda e, cos=cos_ap: e.tensor_mul(
                                     sb_t1[:], sb_qf[:], cos), inc=sV)
                            P.op("vector",
                                 lambda e, sin=sin_ap: e.tensor_mul(
                                     sb_t2[:], ps["r"][:], sin),
                                 wait=[(sT, rv)], inc=sV)
                            G["rotb"] = sV.n
                            P.op("vector",
                                 lambda e, d=dst: e.tensor_add(d, sb_t1[:],
                                                               sb_t2[:]),
                                 inc=sV)
                            G["ropeb"] = G["ropedone"] = G[f"pbv{par}"] = sV.n
                        else:
                            P.op("scalar",
                                 lambda e, p=pb, c=c: e.activation(
                                     sb_qb[:], p[:], AF.Identity,
                                     bias=sb_bqkv[:, c:c + 1], scale=1.0),
                                 wait=[(sT, mmv), (sT, G.get("vtT", 0)),
                                       (sT, G.get("qbT", 0))], inc=sS)
                            G[f"pb{par}"] = ev = sS.n
                            pt = ps["t0"] if hf == 0 else ps["t1"]
                            for tt in range(4):
                                P.op("tensor",
                                     lambda e, pt=pt, tt=tt: e.transpose(
                                         pt[:, tt * 128:(tt + 1) * 128],
                                         sb_qb[:, tt * 128:(tt + 1) * 128],
                                         sb_idb[:]),
                                     wait=([(sS, ev), (sV, G.get(f"vt{hf}", 0)),
                                            (sV, G.get(f"xtp{hf}", 0))]
                                           if tt == 0 else ()),
                                     inc=(sT if tt == 3 else None))
                            G["vtT"] = tpv = sT.n
                            t4 = tok0 // 128
                            dst = sbV[:].rearrange("p (t hc) -> p t hc",
                                                   hc=HPC * 129)[
                                :, t4:t4 + 4, head * 129: head * 129 + 128]
                            src2 = pt[:].rearrange("p (t j) -> p t j", j=128)[
                                :, 0:4, :]
                            P.op("vector",
                                 lambda e, d=dst, ss=src2: e.tensor_copy(d, ss),
                                 wait=[(sT, tpv)], inc=sV)
                            G[f"vt{hf}"] = G[f"pbv{par}"] = sV.n

        def attention(a, sbQ, sbK, sbV, sbAT, sbP, sbSC, ps):
            barrier([("qks0", "sS"), ("qks1", "sS"), ("aev0", "sV"),
                     ("aev1", "sV"), ("atT", "sT"), ("atE", "sV")])
            mask = sb_m1 if a == 1 else sb_m2
            gch = [0]
            IT = [(h, t) for h in range(HPC) for t in range(NT)]
            st = {}

            def emit_qk(idx):
                h, t = IT[idx]
                par = idx % 2
                klen = (t + 1) * 128
                nchunks = (klen + 511) // 512
                lhs = sbQ[:, h * S + t * 128: h * S + (t + 1) * 128]
                evs = []
                for ci in range(nchunks):
                    g = gch[0]; gch[0] += 1
                    pb = ps["a"] if g % 2 == 0 else ps["b"]
                    w = klen - ci * 512 if ci == nchunks - 1 else 512
                    rhs = sbK[:, h * S + ci * 512: h * S + ci * 512 + w]
                    P.op("tensor",
                         lambda e, pb=pb, l=lhs, r=rhs, w=w: e.matmul(
                             pb[:, 0:w], l, r, start=True, stop=True),
                         wait=[(sS, st.get(("ev", g - 2), 0))],
                         inc=sT)
                    mmv = sT.n
                    P.op("scalar",
                         lambda e, pb=pb, ci=ci, w=w, par=par: e.copy(
                             sbSC[par][:, ci * 512: ci * 512 + w], pb[:, 0:w]),
                         wait=[(sT, mmv), (sS, st.get(("rowfree", par), 0))],
                         inc=sS)
                    st[("ev", g)] = sS.n
                    evs.append(sS.n)
                st[("evs", idx)] = evs

            def emit_max(idx):
                h, t = IT[idx]
                par = idx % 2
                klen = (t + 1) * 128
                nchunks = (klen + 511) // 512
                evs = st.pop(("evs", idx))
                for ci in range(nchunks - 1):
                    P.op("vector",
                         lambda e, ci=ci, par=par: e.reduce_max(
                             sb_mx2[par][:, ci:ci + 1],
                             sbSC[par][:, ci * 512: ci * 512 + 512], axis=AX),
                         wait=[(sS, evs[ci])] +
                              ([(sV, st.get(("mxfree", par), 0))]
                               if ci == 0 else []),
                         inc=sV)
                P.op("vector",
                     lambda e, t=t, m=mask, par=par: e.tensor_add(
                         sbSC[par][:, t * 128:(t + 1) * 128],
                         sbSC[par][:, t * 128:(t + 1) * 128], m[:]),
                     wait=[(sS, evs[-1])] +
                          ([(sV, st.get(("mxfree", par), 0))]
                           if nchunks == 1 else []),
                     inc=sV)
                mkv = sV.n
                ci = nchunks - 1
                w = klen - ci * 512
                P.op("vector",
                     lambda e, ci=ci, w=w, par=par: e.reduce_max(
                         sb_mx2[par][:, ci:ci + 1],
                         sbSC[par][:, ci * 512: ci * 512 + w], axis=AX),
                     wait=[(sV, mkv)], inc=sV)
                st[("mx", idx)] = sV.n

            def emit_soft(idx):
                h, t = IT[idx]
                par = idx % 2
                klen = (t + 1) * 128
                nchunks = (klen + 511) // 512
                P.op("vector",
                     lambda e, n=nchunks, par=par: e.reduce_max(
                         sb_nm2[par][:, 0:1], sb_mx2[par][:, 0:n], axis=AX,
                         negate=True),
                     wait=[(sV, st[("mx", idx)])], inc=sV)
                st[("mxfree", par)] = gmv = sV.n
                expch = []
                for ci in range(nchunks):
                    w = klen - ci * 512 if ci == nchunks - 1 else 512
                    P.op("scalar",
                         lambda e, ci=ci, w=w, par=par: e.activation(
                             sbP[par][:, ci * 512: ci * 512 + w],
                             sbSC[par][:, ci * 512: ci * 512 + w],
                             AF.Exp, bias=sb_nm2[par][:, 0:1], scale=1.0),
                         wait=[(sV, gmv), (sT, st.get(("Pfree", par), 0))],
                         inc=sS)
                    expch.append(sS.n)
                st[("rowfree", par)] = sS.n
                st[("soft", idx)] = (expch, 0)

            def emit_pv(idx):
                h, t = IT[idx]
                par = idx % 2
                klen = (t + 1) * 128
                nchunks = (klen + 511) // 512
                expch, rcpv = st.pop(("soft", idx))
                acc = ps["acc0"] if t % 2 == 0 else ps["acc1"]
                nbat = (t + 4) // 4
                ptev = {}

                def tp_batch(b):
                    pt = ps["t0"] if b % 2 == 0 else ps["t1"]
                    n = min(4, (t + 1) - b * 4)
                    for j in range(n):
                        kt = b * 4 + j
                        P.op("tensor",
                             lambda e, pt=pt, j=j, kt=kt, par=par: e.transpose(
                                 pt[:, j * 128:(j + 1) * 128],
                                 sbP[par][:, kt * 128:(kt + 1) * 128], sb_idb[:]),
                             wait=([(sS, expch[min(b, nchunks - 1)]),
                                    ptev.get(b - 2, (sS, 0))]
                                   if j == 0 else ()),
                             inc=(sT if j == n - 1 else None))
                    tv = sT.n
                    if b % 2 == 0:
                        P.op("scalar",
                             lambda e, pt=pt, b=b, n=n: e.copy(
                                 sb_pt[b % 2][:, 0:n * 128], pt[:, 0:n * 128]),
                             wait=[(sT, tv)], inc=sS)
                        ptev[b] = (sS, sS.n)
                    else:
                        P.op("vector",
                             lambda e, pt=pt, b=b, n=n: e.tensor_copy(
                                 sb_pt[b % 2][:, 0:n * 128], pt[:, 0:n * 128]),
                             wait=[(sT, tv)], inc=sV)
                        ptev[b] = (sV, sV.n)

                def pv_batch(b):
                    n = min(4, (t + 1) - b * 4)
                    for j in range(n):
                        kt = b * 4 + j
                        vtile = sbV[:, kt * (HPC * 129) + h * 129:
                                    kt * (HPC * 129) + h * 129 + 129]
                        P.op("tensor",
                             lambda e, acc=acc, sl=sb_pt[b % 2], j=j, v=vtile,
                             stt=(b == 0 and j == 0), sp=(kt == t): e.matmul(
                                 acc[:, 0:129], sl[:, j * 128:(j + 1) * 128],
                                 v, start=stt, stop=sp),
                             wait=([ptev[b]] +
                                   ([(sV, st.get(("aev", t % 2), 0)),
                                     (sV, G["vones"])]
                                    if b == 0 and j == 0 else []))
                             if j == 0 else (),
                             inc=(sT if kt == t else None))

                tp_batch(0)
                for bb in range(1, nbat):
                    tp_batch(bb)
                    pv_batch(bb - 1)
                pv_batch(nbat - 1)
                st[("Pfree", par)] = pvv = sT.n
                P.op("vector",
                     lambda e, acc=acc: e.reciprocal(
                         sb_rcp[:, 0:1], acc[:, 128:129]),
                     wait=[(sT, pvv)], inc=sV)
                rcv = sV.n
                P.op("vector",
                     lambda e, acc=acc: e.tensor_scalar_mul(
                         sb_at[:], acc[:, 0:128], sb_rcp[:, 0:1]),
                     wait=[(sV, rcv), (sT, st.get("atT", 0))],
                     inc=sV)
                st[("aev", t % 2)] = av = sV.n
                P.op("tensor",
                     lambda e: e.transpose(ps["t2"][:, 0:128], sb_at[:],
                                           sb_idb[:]),
                     wait=[(sV, av), (sV, st.get("atE", 0))], inc=sT)
                st["atT"] = atv = sT.n
                P.op("vector",
                     lambda e, h=h, t=t, d=sbAT: e.tensor_copy(
                         d[:, h * S + t * 128: h * S + (t + 1) * 128],
                         ps["t2"][:, 0:128]),
                     wait=[(sT, atv)], inc=sV)
                st["atE"] = sV.n

            for idx in range(len(IT) + 1):
                if idx >= 1:
                    emit_soft(idx - 1)
                if idx < len(IT):
                    emit_qk(idx)
                    emit_max(idx)
                if idx >= 1:
                    emit_pv(idx - 1)

        def wo_partial(a, sbAT, ps, scope):
            barrier([("wp0", "sS"), ("wp1", "sS")])
            G["wstq0"] = qGs[0].n
            G["wstq1"] = qGs[1].n
            A = scope.enter_context
            wsrc = wom1 if a == 1 else wom2
            rsin = rs1_in if a == 1 else rs3_in
            sb_wom = A(nc.sbuf_tensor(f"sb_wom{a}", [128, HPC * D], BF16))
            sb_wst = [A(nc.sbuf_tensor(f"sb_wst{a}{i}", [128, 4096], BF16))
                      for i in range(2)]
            assert nc.sbuf_bytes_remaining >= 0, "sbuf overflow wo_partial"
            for dc in range(HPC):
                src = wsrc[dc * 128:(dc + 1) * 128, :]
                dst = sb_wom[:, dc * D:(dc + 1) * D]
                P.op("gpsimd",
                     lambda e, d=dst, s=src: e.dma_start(out=d, in_=s),
                     inc=qW[0], incn=16)
            womv = qW[0].n
            atv = sV.n
            for i in range(ND):
                sl = i % 2
                for rng in range(8):
                    par = rng % 2
                    pb = ps["a"] if par == 0 else ps["b"]
                    for dc in range(HPC):
                        lhs_ap = sb_wom[:, dc * D + i * 128: dc * D + (i + 1) * 128]
                        rhs = sbAT[:, dc * S + rng * 512: dc * S + (rng + 1) * 512]
                        P.op("tensor",
                             lambda e, pb=pb, l=lhs_ap, r=rhs, dc=dc: e.matmul(
                                 pb[:], l, r, start=(dc == 0), stop=(dc == HPC - 1)),
                             wait=([(qW[0], womv), (sV, atv),
                                    (sS, G.get(f"wp{par}", 0))] if dc == 0 else ()),
                             inc=(sT if dc == HPC - 1 else None))
                    mmv = sT.n
                    P.op("scalar",
                         lambda e, pb=pb, sl=sl, rng=rng: e.copy(
                             sb_wst[sl][:, rng * 512:(rng + 1) * 512], pb[:]),
                         wait=[(sT, mmv)] +
                              ([(qGs[sl], G.get(f"wstq{sl}", 0))] if rng == 0
                               else []),
                         inc=sS)
                    G[f"wp{par}"] = sS.n
                stv = sS.n
                src = sb_wst[sl][:].rearrange("p (q f) -> p q f", q=4)
                dst = rsin[i // 4, :, (i % 4) * 128:(i % 4 + 1) * 128, :]
                dst = dst.rearrange("q p f -> p q f")
                P.op("sync",
                     lambda e, d=dst, s=src: e.dma_start(out=d, in_=s),
                     wait=[(sS, stv)], inc=qGs[sl], incn=16)
                G[f"wstq{sl}"] = qGs[sl].n
                if i % 4 == 3:
                    G[f"wosub{i // 4}"] = [(qGs[0], qGs[0].n),
                                           (qGs[1], qGs[1].n)]

        def collective(kind, cin, cout, key, subwaits=None):
            vals = []
            for g in range(4):
                if kind == "ReduceScatter":
                    ins = cin[g]
                    outs = cout[g * 512:(g + 1) * 512, :]
                else:
                    ins = cin[g * 512:(g + 1) * 512, :]
                    outs = cout[g]
                w = (subwaits[g] if subwaits else
                     [(qAG, qAG.n), (qGs[0], qGs[0].n), (qGs[1], qGs[1].n),
                      (qSp[0], qSp[0].n), (qSp[1], qSp[1].n)])
                P.op("gpsimd",
                     lambda e, ii=ins, oo=outs, k=kind: e.collective_compute(
                         k, ALU.add if k == "ReduceScatter" else ALU.bypass,
                         replica_groups=GROUPS,
                         ins=[ii.opt()], outs=[oo.opt()]),
                     wait=w, inc=cc)
                vals.append(cc.n)
            G[key] = vals

        def res_ln(which, scope):
            barrier([("rsb0", "sS"), ("rsb1", "sS"), ("chq0", "qS0"),
                     ("chq1", "qS1"), ("chu0", "sV"), ("chu1", "sV"),
                     ("xtT", "sT"), ("xev", "sV"), ("h1u0", "sV"),
                     ("h1u1", "sV"), ("statfree", "sV"), ("bcfree", "sV"),
                     ("stT", "sT")], nocc=True)
            A = scope.enter_context
            rsout = rs1_out if which == 1 else rs3_out
            spill = h1_sp if which == 1 else h_sp
            sb_ch = [A(nc.sbuf_tensor(f"sb_ch{which}{i}", [128, QT], F32))
                     for i in range(2)]
            sb_hf = A(nc.sbuf_tensor(f"sb_hf{which}", [128, ND * QT], F32))
            sb_h1s = [A(nc.sbuf_tensor(f"sb_h1s{which}{i}", [128, QT], F32))
                      for i in range(2)]
            sb_rsb = [A(nc.sbuf_tensor(f"sb_rsb{which}{i}", [128, QT], BF16))
                      for i in range(2)]
            ps_x = A(nc.psum_tensor(f"r{which}_x", [128, 512], F32))
            ps_s = [A(nc.psum_tensor(f"r{which}_s{i}", [128, 512], F32))
                    for i in range(4)]
            ps_bc = A(nc.psum_tensor(f"r{which}_bc", [128, 512], F32))
            G["sb_bcm"] = [A(nc.sbuf_tensor(f"sb_bcm{which}{i}", [128, 512], F32))
                           for i in range(2)]
            G["sb_bci"] = [A(nc.sbuf_tensor(f"sb_bci{which}{i}", [128, 512], F32))
                           for i in range(2)]
            stp = tuple(ps_s)
            assert nc.sbuf_bytes_remaining >= 0, "sbuf overflow res_ln"

            if which == 2:
                P.op("sync", lambda e: e.dma_start(out=sb_bo[:], in_=bo2[:, :]),
                     wait=[(sS, sS.n)], inc=qX[0], incn=16)
            bov = qX[0].n
            ccn = G["cc_rs1"] if which == 1 else G["cc_rs3"]

            if which == 1:
                sb_xs = [A(nc.sbuf_tensor(f"sb_xs{i}", [128, D], F32))
                         for i in range(2)]
                sb_xmT = A(nc.sbuf_tensor("sb_xmT", [128, ND * QT], F32))
                assert nc.sbuf_bytes_remaining >= 0, "sbuf overflow xprep"
                for tt in range(8):
                    xsl = tt % 2
                    P.op("sync",
                         lambda e, tt=tt, xsl=xsl: e.dma_start(
                             out=sb_xs[xsl][:],
                             in_=x_mine[tt * 128:(tt + 1) * 128, :]),
                         wait=[(sT, G.get(f"xpt{xsl}", 0))],
                         inc=qH[xsl], incn=16)
                    xv = qH[xsl].n
                    for g4 in range(4):
                        for j in range(4):
                            dc = g4 * 4 + j
                            P.op("tensor",
                                 lambda e, xsl=xsl, dc=dc, j=j: e.transpose(
                                     ps_x[:, j * 128:(j + 1) * 128],
                                     sb_xs[xsl][:, dc * 128:(dc + 1) * 128],
                                     sb_idf[:]),
                                 wait=([(qH[xsl], xv),
                                        (sV, G.get("xpe", 0))] if j == 0 else ()),
                                 inc=(sT if j == 3 else None))
                        tv = sT.n
                        dst = sb_xmT[:].rearrange(
                            "p (dc t) -> p dc t", t=QT)[
                            :, g4 * 4:(g4 + 1) * 4, tt * 128:(tt + 1) * 128]
                        src2 = ps_x[:].rearrange("p (j c) -> p j c", c=128)
                        P.op("vector",
                             lambda e, d=dst, ss=src2: e.tensor_copy(d, ss),
                             wait=[(sT, tv)], inc=sV)
                        G["xpe"] = sV.n
                    G[f"xpt{xsl}"] = sT.n
                xprep_done = sV.n

            for i in range(ND):
                sl = i % 2
                P.op("sync",
                     lambda e, i=i, sl=sl, r=rsout: e.dma_start(
                         out=sb_rsb[sl][:], in_=r[i * 128:(i + 1) * 128, :]),
                     wait=[(cc, ccn[i // 4]), (sS, G.get(f"rsb{sl}", 0))],
                     inc=qRS[sl], incn=16)
                rv = qRS[sl].n
                h1v = 0
                if which == 2:
                    P.op("sync",
                         lambda e, i=i, sl=sl: e.dma_start(
                             out=sb_h1s[sl][:], in_=h1_sp[i * 128:(i + 1) * 128, :]),
                         wait=[(sV, G.get(f"h1u{sl}", 0))], inc=qH[sl], incn=16)
                    h1v = qH[sl].n
                chb = sb_hf[:, i * QT:(i + 1) * QT]
                P.op("scalar",
                     lambda e, d=chb, sl=sl, i=i: e.activation(
                         d, sb_rsb[sl][:], AF.Identity,
                         bias=sb_bo[:, i:i + 1], scale=1.0),
                     wait=[(qRS[sl], rv), (qX[0], bov)],
                     inc=sS)
                G[f"rsb{sl}"] = evv = sS.n
                if which == 1:
                    P.op("vector",
                         lambda e, d=chb, i=i: e.tensor_add(
                             d, d, sb_xmT[:, i * QT:(i + 1) * QT]),
                         wait=[(sS, evv), (sV, xprep_done)], inc=sV)
                    chv = sV.n
                else:
                    P.op("vector",
                         lambda e, d=chb, s2=sb_h1s[sl]: e.tensor_add(
                             d, d, s2[:]),
                         wait=[(sS, evv), (qH[sl], h1v)], inc=sV)
                    G[f"h1u{sl}"] = chv = sV.n
                if which == 2:
                    P.op("scalar",
                         lambda e, d=chb, i=i, sp=spill: e.dma_start(
                             out=sp[i * 128:(i + 1) * 128, :], in_=d),
                         wait=[(sV, chv)], inc=qSp[sl], incn=16, dma=True)
                    G[f"chq{sl}"] = qSp[sl].n
                cbv, sqv = ln_stats_chunk(stp, i, chb, (sV, chv))
                G[f"chu{sl}"] = max(cbv, 0)

            ivv = ln_finish(stp)

            if which == 1:
                bcv = emit_bcasts(ps_bc, ivv)
                s0 = sS.n
                for i in range(ND):
                    sl = i % 2
                    chb = sb_hf[:, i * QT:(i + 1) * QT]
                    nv = norm_apply(chb, chb, bcv, extra_wait=[(sS, s0)])
                    P.op("scalar",
                         lambda e, d=chb, i=i: e.dma_start(
                             out=h1_sp[i * 128:(i + 1) * 128, :], in_=d),
                         wait=[(sV, nv)], inc=qSp[sl], incn=16, dma=True)
                    G[f"n1q{sl}"] = qSp[sl].n
                    cbv, sqv = ln_stats_chunk(stp, i, chb, (sV, nv))
                iv2 = ln_finish(stp)
                bcv = emit_bcasts(ps_bc, iv2)
                s0 = sS.n
                for i in range(ND):
                    sl = i % 2
                    chb = sb_ch[sl]
                    src_ap = sb_hf[:, i * QT:(i + 1) * QT]
                    nv = norm_apply(chb[:], src_ap, bcv,
                                    extra_wait=[(sS, s0),
                                                (sS, G.get(f"n2s{sl}", 0))])
                    G[f"n2u{sl}"] = nv
                    P.op("scalar",
                         lambda e, d=chb, i=i: e.activation(
                             sb_cb[:], d[:], AF.Identity,
                             bias=sb_ln1b[:, i:i + 1], scale=sb_ln1w[:, i:i + 1]),
                         wait=[(sV, nv), (qAG, G.get("ag2q", 0))],
                         inc=sS)
                    G[f"n2s{sl}"] = xv = sS.n
                    P.op("scalar",
                         lambda e, i=i: e.dma_start(
                             out=ag2_in[i * 128:(i + 1) * 128, :], in_=sb_cb[:]),
                         wait=[(sS, xv)], inc=qAG, incn=16, dma=True)
                    G["ag2q"] = qAG.n
                    if i % 4 == 3:
                        G[f"agsub{i // 4}"] = [(qAG, qAG.n)]
            else:
                bcv = emit_bcasts(ps_bc, ivv)
                s0 = sS.n
                G["x2q0"] = qSp[0].n
                G["x2q1"] = qSp[1].n
                for i in range(ND):
                    sl = i % 2
                    chb = sb_ch[sl]
                    src_ap = sb_hf[:, i * QT:(i + 1) * QT]
                    nv = norm_apply(chb[:], src_ap, bcv,
                                    extra_wait=[(sS, s0),
                                                (sS, G.get(f"x2s{sl}", 0)),
                                                (qSp[sl], G.get(f"x2q{sl}", 0))])
                    G[f"n3u{sl}"] = nv
                    P.op("scalar",
                         lambda e, d=chb, sl=sl, i=i: e.activation(
                             sb_ch[1 - sl][:], d[:], AF.Identity,
                             bias=sb_ln2b[:, i:i + 1], scale=sb_ln2w[:, i:i + 1]),
                         wait=[(sV, nv), (qSp[1 - sl], G.get(f"x2q{1 - sl}", 0)),
                               (sV, G.get(f"x2u{1 - sl}", 0))],
                         inc=sS)
                    G[f"x2s{1 - sl}"] = xv = sS.n
                    P.op("scalar",
                         lambda e, sl=sl, i=i: e.dma_start(
                             out=xg_sp[i * 128:(i + 1) * 128, :],
                             in_=sb_ch[1 - sl][:]),
                         wait=[(sS, xv)], inc=qSp[1 - sl], incn=16, dma=True)
                    G[f"x2q{1 - sl}"] = qSp[1 - sl].n
                    P.op("vector",
                         lambda e, sl=sl, i=i: e.tensor_copy(
                             G["sb_xln2T"][:, i * QT:(i + 1) * QT],
                             sb_ch[1 - sl][:]),
                         wait=[(sS, xv)], inc=sV)
                    G[f"x2u{1 - sl}"] = sV.n

        def gating(scope):
            barrier([("gxT", "sT"), ("gev", "sV"), ("cfmv", "sV"),
                     ("cbbv", "sV")])
            A = scope.enter_context
            sb_gt = A(nc.sbuf_tensor("sb_gt", [128, 32], F32))
            sb_gr = A(nc.sbuf_tensor("sb_gr", [128, 4], F32))
            sb_gm = A(nc.sbuf_tensor("sb_gm", [128, 4], F32))
            sb_gc = A(nc.sbuf_tensor("sb_gc", [128, 10], F32))
            sb_gx = A(nc.sbuf_tensor("sb_gx", [128, 128], F32))
            sb_cbf = A(nc.sbuf_tensor("sb_cbf", [128, 4], BF16))
            sb_cfm = A(nc.sbuf_tensor("sb_cfm", [4, QT], BF16))
            ps_g = A(nc.psum_tensor("ps_g", [128, 512], F32))
            ps_gt = A(nc.psum_tensor("ps_gt", [128, 1024], BF16))
            ps_cb = A(nc.psum_tensor("ps_cb", [128, 512], F32))
            assert nc.sbuf_bytes_remaining >= 0, "sbuf overflow gating"

            P.op("gpsimd",
                 lambda e: e.dma_start(out=G["sb_be2r"][0:1, :], in_=be2[:, :]),
                 inc=qW[0], incn=16)
            G["be2rq"] = qW[0].n

            for tt in range(8):
                for dc in range(ND):
                    P.op("sync",
                         lambda e, tt=tt, dc=dc: e.dma_start(
                             out=sb_gx[:],
                             in_=xg_sp[dc * 128:(dc + 1) * 128,
                                       tt * 128:(tt + 1) * 128]),
                         wait=[(sT, G.get("gxT", 0)), (qSp[0], qSp[0].n), (qSp[1], qSp[1].n)],
                         inc=qX[1], incn=16)
                    gxv = qX[1].n
                    P.op("tensor",
                         lambda e, dc=dc: e.matmul(
                             ps_g[:, 0:4], sb_gx[:], sb_wg[:, dc * 4:(dc + 1) * 4],
                             start=(dc == 0), stop=(dc == ND - 1)),
                         wait=[(qX[1], gxv)] +
                              ([(sV, G.get("gev", 0))] if dc == 0 else []),
                         inc=sT)
                    G["gxT"] = sT.n
                mmv = sT.n
                P.op("vector",
                     lambda e, tt=tt: e.tensor_copy(
                         sb_gt[:, tt * 4:(tt + 1) * 4], ps_g[:, 0:4]),
                     wait=[(sT, mmv)], inc=sV)
                G["gev"] = sV.n

            for tt in range(8):
                gsl = sb_gt[:, tt * 4:(tt + 1) * 4]
                P.op("vector", lambda e, g=gsl: e.reduce_max(
                    sb_gc[:, 0:1], g, axis=AX, negate=True), inc=sV)
                vv = sV.n
                P.op("scalar", lambda e, g=gsl: e.activation(
                    sb_gr[:], g, AF.Exp, bias=sb_gc[:, 0:1], scale=1.0,
                    accum_out=sb_gc[:, 1:2]),
                    wait=[(sV, vv)], inc=sS)
                ev = sS.n
                P.op("vector", lambda e: e.reciprocal(sb_gc[:, 2:3], sb_gc[:, 1:2]),
                     wait=[(sS, ev)], inc=sV)
                P.op("vector", lambda e: e.tensor_scalar_mul(
                    sb_gr[:], sb_gr[:], sb_gc[:, 2:3]), inc=sV)
                pairs = [(0, 1), (0, 2), (0, 3), (1, 2), (1, 3), (2, 3)]
                for k, (pa, pb_) in enumerate(pairs):
                    P.op("vector",
                         lambda e, pa=pa, pb_=pb_, k=k: e.tensor_tensor(
                             sb_gc[:, 3 + k:4 + k], sb_gr[:, pa:pa + 1],
                             sb_gr[:, pb_:pb_ + 1], ALU.is_ge), inc=sV)
                bb = [sb_gc[:, 3 + k:4 + k] for k in range(6)]
                cnt = sb_gc[:, 9:10]
                plans = [
                    ((bb[0], -1.0, 3.0), [(-1, bb[1]), (-1, bb[2])]),
                    ((bb[3], -1.0, 2.0), [(1, bb[0]), (-1, bb[4])]),
                    ((bb[5], -1.0, 1.0), [(1, bb[1]), (1, bb[3])]),
                    ((bb[2], 1.0, 0.0), [(1, bb[4]), (1, bb[5])]),
                ]
                for eidx, ((t0, m0, a0), rest) in enumerate(plans):
                    P.op("vector",
                         lambda e, t0=t0, m0=m0, a0=a0: e.tensor_scalar(
                             cnt, t0, m0, a0, ALU.mult, ALU.add), inc=sV)
                    for sgn, tap in rest:
                        op = ALU.add if sgn > 0 else ALU.subtract
                        P.op("vector",
                             lambda e, tap=tap, op=op: e.tensor_tensor(
                                 cnt, cnt, tap, op), inc=sV)
                    P.op("vector",
                         lambda e, eidx=eidx: e.tensor_single_scalar(
                             sb_gm[:, eidx:eidx + 1], cnt, 1.5, ALU.is_lt), inc=sV)
                vv = sV.n
                P.op("scalar", lambda e: e.activation(sb_gr[:], sb_gr[:], AF.Exp),
                     wait=[(sV, vv)], inc=sS)
                ev2 = sS.n
                P.op("vector", lambda e: e.tensor_mul(sb_gr[:], sb_gr[:], sb_gm[:]),
                     wait=[(sS, ev2)], inc=sV)
                P.op("vector", lambda e: e.reduce_sum(sb_gc[:, 0:1], sb_gr[:],
                                                      axis=AX), inc=sV)
                P.op("vector", lambda e: e.reciprocal(sb_gc[:, 1:2], sb_gc[:, 0:1]),
                     inc=sV)
                P.op("vector", lambda e: e.tensor_scalar_mul(
                    sb_gr[:], sb_gr[:], sb_gc[:, 1:2]), inc=sV)
                P.op("vector", lambda e: e.tensor_copy(sb_cbf[:], sb_gr[:]), inc=sV)
                cv = sV.n
                po = (tt % 2) * 128
                P.op("tensor",
                     lambda e, po=po: e.transpose(
                         ps_gt[0:4, po:po + 128], sb_cbf[:], sb_idb[:]),
                     wait=[(sV, cv), (sV, G.get("cfmv", 0))], inc=sT)
                tv = sT.n
                P.op("vector",
                     lambda e, tt=tt, po=po: e.tensor_copy(
                         sb_cfm[0:4, tt * 128:(tt + 1) * 128],
                         ps_gt[0:4, po:po + 128]),
                     wait=[(sT, tv)], inc=sV)
                G["cfmv"] = sV.n
            cfmv = sV.n
            P.op("sync",
                 lambda e: e.dma_start(out=G["sb_combr"][0:1, :], in_=sb_cfm[:, :]),
                 wait=[(sV, cfmv)], inc=qX[0], incn=16)
            crv = qX[0].n
            for ee in range(NE):
                for hh in range(2):
                    P.op("tensor",
                         lambda e, ee=ee, hh=hh: e.matmul(
                             ps_cb[:, 0:512], sb_onesrb[0:1, :],
                             G["sb_combr"][0:1, ee * QT + hh * 512:
                                           ee * QT + (hh + 1) * 512],
                             start=True, stop=True),
                         wait=[(qX[0], crv), (sV, G.get("cbbv", 0))], inc=sT)
                    tv = sT.n
                    P.op("vector",
                         lambda e, ee=ee, hh=hh: e.tensor_copy(
                             G["sb_combb"][:, ee * QT + hh * 512:
                                           ee * QT + (hh + 1) * 512],
                             ps_cb[:, 0:512]),
                         wait=[(sT, tv)], inc=sV)
                    G["cbbv"] = sV.n
            G["combv"] = sV.n

        def experts(scope, sb_xln2T, sb_heT, sb_m, sb_wc):
            barrier([("ex0", "sS"), ("ex1", "sS"), ("exv0", "sV"),
                     ("exv1", "sV"), ("heV", "sV")], skip_engs=("gpsimd",))
            acc = [scope.enter_context(nc.psum_tensor(f"ps_e{i}", [128, 512], F32))
                   for i in range(8)]
            wload = {}

            def load_quarter(e_i, li, q4):
                wsrc = we1 if li == 0 else we2
                sl = q4 % 2
                vals = []
                for dc in range(ND):
                    qq = qE[sl * 4 + dc % 4]
                    src = wsrc[e_i, dc * 128:(dc + 1) * 128,
                               q4 * 512:(q4 + 1) * 512]
                    dst = sb_wc[sl][:, dc * 512:(dc + 1) * 512]
                    P.op("gpsimd",
                         lambda e, d=dst, ss=src: e.dma_start(out=d, in_=ss),
                         wait=[(sT, G.get(f"mwq{sl}", 0)), (qq, qq.n)],
                         inc=qq, incn=16)
                    vals.append((qq, qq.n))
                wload[(e_i, li, q4)] = vals

            seqs = [(e_i, li, q4) for e_i in range(NE) for li in range(2)
                    for q4 in range(4)]
            load_quarter(*seqs[0])
            load_quarter(*seqs[1])

            for si, (e_i, li, q4) in enumerate(seqs):
                rhs_sb = sb_xln2T if li == 0 else sb_heT
                sl = q4 % 2
                vals = wload.pop((e_i, li, q4))
                for pp in range(2):
                    p_i = q4 * 2 + pp
                    st = p_i % 2
                    bs = acc[st * 4:(st + 1) * 4]
                    for dc in range(ND):
                        for tg in range(2):
                            for ii in range(2):
                                pb = bs[ii * 2 + tg]
                                w = []
                                if tg == 0 and ii == 0:
                                    w.append(vals[dc])
                                    if dc == 0:
                                        w += [(sS, G.get(f"ex{st}", 0)),
                                              (sV, G.get(f"exv{st}", 0)),
                                              (sV, G.get("combv", 0))]
                                        if li == 1:
                                            w.append((sV, G.get("heV", 0)))
                                P.op("tensor",
                                     lambda e, pb=pb, sl=sl, dc=dc, pp=pp,
                                     ii=ii, tg=tg, li=li, rs=rhs_sb: e.matmul(
                                         pb[:],
                                         sb_wc[sl][:, dc * 512 + pp * 256 +
                                                   ii * 128: dc * 512 +
                                                   pp * 256 + (ii + 1) * 128],
                                         rs[:, dc * QT + tg * 512:
                                            dc * QT + (tg + 1) * 512],
                                         start=(dc == 0),
                                         stop=(dc == ND - 1 and li == 0)),
                                     wait=w,
                                     inc=(sT if tg == 1 and ii == 1 else None))
                    if li == 1:
                        for tg in range(2):
                            for ii in range(2):
                                pb = bs[ii * 2 + tg]
                                col = p_i * 2 + ii
                                l_ap = G["sb_be2r"][0:1,
                                                    e_i * D + col * 128:
                                                    e_i * D + (col + 1) * 128]
                                r_ap = G["sb_combr"][0:1,
                                                     e_i * QT + tg * 512:
                                                     e_i * QT + (tg + 1) * 512]
                                P.op("tensor",
                                     lambda e, pb=pb, l=l_ap, r=r_ap: e.matmul(
                                         pb[:], l, r, start=False, stop=True),
                                     inc=(sT if tg == 1 and ii == 1 else None))
                    mmv = sT.n
                    if pp == 1:
                        G[f"mwq{sl}"] = mmv
                        if si + 2 < len(seqs):
                            load_quarter(*seqs[si + 2])
                    for tg in range(2):
                        for ii in range(2):
                            pb = bs[ii * 2 + tg]
                            col = p_i * 2 + ii
                            if li == 0:
                                dst = sb_heT[:, col * QT + tg * 512:
                                             col * QT + (tg + 1) * 512]
                                P.op("scalar",
                                     lambda e, pb=pb, d=dst, col=col, e_i=e_i:
                                     e.activation(
                                         d, pb[:], AF.Gelu,
                                         bias=sb_be1[:, e_i * ND + col:
                                                     e_i * ND + col + 1],
                                         scale=1.0),
                                     wait=[(sT, mmv)], inc=sS)
                                gv = sS.n
                                cb_ap = G["sb_combb"][
                                    :, e_i * QT + tg * 512:
                                    e_i * QT + (tg + 1) * 512]
                                P.op("vector",
                                     lambda e, d=dst, cb=cb_ap: e.tensor_mul(
                                         d, d, cb),
                                     wait=[(sS, gv)], inc=sV)
                            else:
                                dst = sb_m[:, col * QT + tg * 512:
                                           col * QT + (tg + 1) * 512]
                                if e_i == 0:
                                    P.op("vector",
                                         lambda e, pb=pb, d=dst: e.tensor_copy(
                                             d, pb[:]),
                                         wait=[(sT, mmv)], inc=sV)
                                else:
                                    P.op("vector",
                                         lambda e, pb=pb, d=dst: e.tensor_add(
                                             d, d, pb[:]),
                                         wait=[(sT, mmv)], inc=sV)
                    G[f"ex{st}"] = sS.n
                    G[f"exv{st}"] = sV.n
                if li == 0 and q4 == 3:
                    G["heV"] = sV.n

        def final_ln(scope, sb_m):
            barrier([("fs0", "sV"), ("fs1", "sV"), ("oev", "sV"),
                     ("statfree", "sV"), ("bcfree", "sV"), ("stT", "sT")])
            A = scope.enter_context
            sb_s = A(nc.sbuf_tensor("sb_s", [128, ND * QT], F32))
            sb_stg = A(nc.sbuf_tensor("sb_stg", [128, D], F32))
            sb_stg2 = A(nc.sbuf_tensor("sb_stg2", [128, D], F32))
            sb_ch = [A(nc.sbuf_tensor(f"sb_fch{i}", [128, QT], F32))
                     for i in range(2)]
            sb_h1s = [A(nc.sbuf_tensor(f"sb_fh1{i}", [128, QT], F32))
                      for i in range(2)]
            ps_x = A(nc.psum_tensor("pf_x", [128, 512], F32))
            ps_s = [A(nc.psum_tensor(f"pf_s{i}", [128, 512], F32)) for i in range(4)]
            ps_bc = A(nc.psum_tensor("pf_bc", [128, 512], F32))
            G["sb_bcm"] = [A(nc.sbuf_tensor(f"sb_bcmf{i}", [128, 512], F32))
                           for i in range(2)]
            G["sb_bci"] = [A(nc.sbuf_tensor(f"sb_bcif{i}", [128, 512], F32))
                           for i in range(2)]
            stp = tuple(ps_s)
            assert nc.sbuf_bytes_remaining >= 0, "sbuf overflow final"

            for i in range(ND):
                sl = i % 2
                ssl = sb_s[:, i * QT:(i + 1) * QT]
                P.op("sync",
                     lambda e, i=i, sl=sl: e.dma_start(
                         out=sb_h1s[sl][:], in_=h_sp[i * 128:(i + 1) * 128, :]),
                     wait=[(sV, G.get(f"fs{sl}", 0)), (qSp[0], qSp[0].n), (qSp[1], qSp[1].n)],
                     inc=qH[sl], incn=16)
                P.op("sync",
                     lambda e, i=i, sl=sl: e.dma_start(
                         out=sb_ch[sl][:], in_=h1_sp[i * 128:(i + 1) * 128, :]),
                     wait=[(sV, G.get(f"fs{sl}", 0))], inc=qH[sl], incn=16)
                h1v = qH[sl].n
                P.op("vector",
                     lambda e, d=ssl, sl=sl: e.tensor_add(
                         d, sb_h1s[sl][:], sb_ch[sl][:]),
                     wait=[(qH[sl], h1v)], inc=sV)
                P.op("vector",
                     lambda e, d=ssl, i=i, m=sb_m: e.tensor_add(
                         d, d, m[:, i * QT:(i + 1) * QT]), inc=sV)
                G[f"fs{sl}"] = svv = sV.n
                ln_stats_chunk(stp, i, ssl, (sV, svv))
            ivv = ln_finish(stp)
            bcv = emit_bcasts(ps_bc, ivv)
            for i in range(ND):
                ssl = sb_s[:, i * QT:(i + 1) * QT]
                norm_apply(ssl, ssl, bcv)
            nv = sV.n
            pxb = [ps_x, ps_s[0]]
            stg2 = [sb_stg, sb_stg2]
            oevk = {}
            for tt in range(8):
                sg = stg2[tt % 2]
                for i in range(ND):
                    k = tt * ND + i
                    pb = pxb[k % 2]
                    src = sb_s[:, i * QT + tt * 128: i * QT + (tt + 1) * 128]
                    P.op("tensor",
                         lambda e, pb=pb, src=src: e.transpose(
                             pb[:, 0:128], src, sb_idf[:]),
                         wait=[(sV, nv), (sV, oevk.get(k - 2, 0))], inc=sT)
                    tv = sT.n
                    P.op("vector",
                         lambda e, sg=sg, pb=pb, i=i: e.tensor_copy(
                             sg[:, i * 128:(i + 1) * 128], pb[:, 0:128]),
                         wait=[(sT, tv)] +
                              ([(qO, G.get(f"stgq{tt % 2}", 0))]
                               if i == 0 else []),
                         inc=sV)
                    oevk[k] = sV.n
                ovv = sV.n
                P.op("sync",
                     lambda e, tt=tt, sg=sg: e.dma_start(
                         out=out_mine[tt * 128:(tt + 1) * 128, :], in_=sg[:]),
                     wait=[(sV, ovv), (qO, qO.n)], inc=qO, incn=16, dma=True)
                G[f"stgq{tt % 2}"] = qO.n
            P.op("sync", None, wait=[(qO, qO.n)])
            P.op("gpsimd", None, wait=[(qO, qO.n)])

        with ExitStack() as sc1:
            A1 = sc1.enter_context
            sbQ = A1(nc.sbuf_tensor("sbQ1", [128, HPC * S], BF16))
            sbK = A1(nc.sbuf_tensor("sbK1", [128, HPC * S], BF16))
            sbV = A1(nc.sbuf_tensor("sbV1", [128, NT * HPC * 129], BF16))
            sbP1 = [A1(nc.sbuf_tensor(f"sbP1{i}", [128, S], BF16))
                    for i in range(2)]
            ps = {k: A1(nc.psum_tensor(f"a1{k}", [128, 512], F32))
                  for k in ("a", "b", "acc0", "acc1", "r")}
            ps["t0"] = A1(nc.psum_tensor("a1t0", [128, 1024], BF16))
            ps["t1"] = A1(nc.psum_tensor("a1t1", [128, 1024], BF16))
            ps["t2"] = A1(nc.psum_tensor("a1t2", [128, 1024], BF16))
            with ExitStack() as sq:
                qkv_proj(1, sbQ, sbK, sbV, ps, sq)
            sbAT = A1(nc.sbuf_tensor("sbAT1", [128, HPC * S], BF16))
            with ExitStack() as ssc:
                sbSC1 = [ssc.enter_context(
                    nc.sbuf_tensor(f"sbSC1{i}", [128, S], F32))
                    for i in range(2)]
                attention(1, sbQ, sbK, sbV, sbAT, sbP1, sbSC1, ps)
            with ExitStack() as sw:
                wo_partial(1, sbAT, ps, sw)
        collective("ReduceScatter", rs1_in, rs1_out, "cc_rs1",
                   subwaits=[G[f"wosub{g}"] for g in range(4)])
        with ExitStack() as sc2:
            res_ln(1, sc2)
        collective("AllGather", ag2_in, ag2_out, "cc_ag2",
                   subwaits=[G[f"agsub{g}"] for g in range(4)])
        with ExitStack() as sc3:
            A3 = sc3.enter_context
            sbQ = A3(nc.sbuf_tensor("sbQ2", [128, HPC * S], BF16))
            sbK = A3(nc.sbuf_tensor("sbK2", [128, HPC * S], BF16))
            sbV = A3(nc.sbuf_tensor("sbV2", [128, NT * HPC * 129], BF16))
            sbP2 = [A3(nc.sbuf_tensor(f"sbP2{i}", [128, S], BF16))
                    for i in range(2)]
            ps = {k: A3(nc.psum_tensor(f"a2{k}", [128, 512], F32))
                  for k in ("a", "b", "acc0", "acc1", "r")}
            ps["t0"] = A3(nc.psum_tensor("a2t0", [128, 1024], BF16))
            ps["t1"] = A3(nc.psum_tensor("a2t1", [128, 1024], BF16))
            ps["t2"] = A3(nc.psum_tensor("a2t2", [128, 1024], BF16))
            with ExitStack() as sq:
                qkv_proj(2, sbQ, sbK, sbV, ps, sq)
            sbAT = A3(nc.sbuf_tensor("sbAT2", [128, HPC * S], BF16))
            with ExitStack() as ssc:
                sbSC2 = [ssc.enter_context(
                    nc.sbuf_tensor(f"sbSC2{i}", [128, S], F32))
                    for i in range(2)]
                attention(2, sbQ, sbK, sbV, sbAT, sbP2, sbSC2, ps)
            with ExitStack() as sw:
                wo_partial(2, sbAT, ps, sw)
        collective("ReduceScatter", rs3_in, rs3_out, "cc_rs3",
                   subwaits=[G[f"wosub{g}"] for g in range(4)])
        with ExitStack() as sc4:
            A4 = sc4.enter_context
            sb_m = A4(nc.sbuf_tensor("sb_m", [128, ND * QT], BF16))
            with ExitStack() as scm:
                Am = scm.enter_context
                G["sb_xln2T"] = Am(nc.sbuf_tensor("sb_xln2T", [128, ND * QT], BF16))
                with ExitStack() as sc4b:
                    res_ln(2, sc4b)
                G["sb_be2r"] = Am(nc.sbuf_tensor("sb_be2r", [1, NE * D], BF16))
                G["sb_combr"] = Am(nc.sbuf_tensor("sb_combr", [1, NE * QT], BF16))
                G["sb_combb"] = Am(nc.sbuf_tensor("sb_combb", [128, NE * QT], BF16))
                sb_heT = Am(nc.sbuf_tensor("sb_heT", [128, ND * QT], BF16))
                sb_mwc = [Am(nc.sbuf_tensor(f"sb_mwc{i}", [128, ND * 512], BF16))
                          for i in range(2)]
                with ExitStack() as sc5:
                    gating(sc5)
                with ExitStack() as sc6:
                    experts(sc6, G["sb_xln2T"], sb_heT, sb_m, sb_mwc)
            with ExitStack() as sc7:
                final_ln(sc7, sb_m)

        if dbg is not None:
            P.op("sync",
                 lambda e: e.dma_start(out=dbg_out.ap(), in_=dsrc.ap()),
                 wait=[(sS, sS.n), (sV, sV.n), (qSp[0], qSp[0].n), (qSp[1], qSp[1].n), (qAG, qAG.n),
                       (cc, cc.n)],
                 inc=qO, incn=16)
            P.op("sync", None, wait=[(qO, qO.n)])

        block = E(nc.Block())

        def replay(name):
            def section(eng):
                hw = {}
                for fn, w, inc, incn in P.ops[name]:
                    for s_, v in w:
                        if hw.get(id(s_), -1) < v:
                            eng.wait_ge(s_.h, v)
                            hw[id(s_)] = v
                    if fn is not None:
                        r = fn(eng)
                        if inc is not None:
                            r.then_inc(inc.h, incn)
            return section

        block.sync(replay("sync"))
        block.gpsimd(replay("gpsimd"))
        block.vector(replay("vector"))
        block.scalar(replay("scalar"))
        block.tensor(replay("tensor"))

    return nc



def _col(v, nch):
    return np.ascontiguousarray(np.asarray(v, dtype=np.float32)
                                .reshape(nch, 128).T)


def make_in_maps(inputs):
    ii = {k: np.asarray(v) for k, v in inputs.items()}
    x = ii["L0_attn_x"].astype(np.float32, copy=False)
    wqkv = ii["L0_attn_w_qkv"].astype(np.float32, copy=False)
    bqkv = ii["L0_attn_b_qkv"].astype(np.float32, copy=False)
    w_o = ii["L0_attn_w_o"].astype(np.float32, copy=False)
    b_o = ii["L0_attn_b_o"].astype(np.float32, copy=False)
    wq2 = ii["L0_ffn_wq"].astype(np.float32, copy=False)
    wk2 = ii["L0_ffn_wk"].astype(np.float32, copy=False)
    wv2 = ii["L0_ffn_wv"].astype(np.float32, copy=False)
    wo2 = ii["L0_ffn_wo"].astype(np.float32, copy=False)
    bq2, bk2, bv2 = (ii["L0_ffn_bq"], ii["L0_ffn_bk"], ii["L0_ffn_bv"])
    maps = []
    for c in range(8):
        b, p = c // 4, c % 4
        heads = list(range(4 * p, 4 * p + 4))
        w1 = np.concatenate([wqkv[:, 384 * h:384 * (h + 1)] for h in heads], axis=1)
        b1 = np.concatenate([bqkv[384 * h:384 * (h + 1)] for h in heads])
        w2cols, b2cols = [], []
        for h in heads:
            sl = slice(128 * h, 128 * (h + 1))
            w2cols += [wq2[:, sl], wk2[:, sl], wv2[:, sl]]
            b2cols += [np.asarray(bq2)[sl], np.asarray(bk2)[sl],
                       np.asarray(bv2)[sl]]
        w2 = np.concatenate(w2cols, axis=1)
        b2 = np.concatenate(b2cols)
        m = {
            "x_full": np.ascontiguousarray(x[b]),
            "x_mine": np.ascontiguousarray(x[b, QT * p:QT * (p + 1)]),
            "wqkv1": np.ascontiguousarray(w1),
            "bqkv1": _col(b1, NCH),
            "wom1": np.ascontiguousarray(w_o[512 * p:512 * (p + 1)]),
            "bo1": _col(b_o, ND),
            "wqkv2": np.ascontiguousarray(w2),
            "bqkv2": _col(b2, NCH),
            "wom2": np.ascontiguousarray(wo2[512 * p:512 * (p + 1)]),
            "bo2": _col(ii["L0_ffn_bo"], ND),
            "ln1w": _col(ii["L0_ffn_ln1_w"], ND),
            "ln1b": _col(ii["L0_ffn_ln1_b"], ND),
            "ln2w": _col(ii["L0_ffn_ln2_w"], ND),
            "ln2b": _col(ii["L0_ffn_ln2_b"], ND),
            "wgate": np.ascontiguousarray(
                ii["L0_ffn_w_gate"].astype(np.float32, copy=False)
                .reshape(ND, 128, NE).transpose(1, 0, 2).reshape(128, ND * NE)),
            "we1": np.ascontiguousarray(ii["L0_ffn_we1"].astype(np.float32,
                                                               copy=False)),
            "be1": np.ascontiguousarray(
                ii["L0_ffn_be1"].astype(np.float32, copy=False)
                .reshape(NE, ND, 128).transpose(2, 0, 1).reshape(128, NE * ND)),
            "we2": np.ascontiguousarray(ii["L0_ffn_we2"].astype(np.float32,
                                                               copy=False)),
            "be2": np.ascontiguousarray(
                ii["L0_ffn_be2"].astype(np.float32, copy=False).reshape(64, 128)),
        }
        maps.append(m)
    return maps


def kernel(**inputs):
    maps = make_in_maps(inputs)
    nc = build_nc()
    res = run_bass_kernel_spmd(nc, maps, core_ids=list(range(8)))
    out = np.empty((B, S, D), dtype=np.float32)
    for c in range(8):
        b, p = c // 4, c % 4
        out[b, QT * p:QT * (p + 1)] = res.results[c]["out_mine"]
    return out
